# version 08
# baseline (speedup 1.0000x reference)
import contextlib
import os as _os
import numpy as np
import ml_dtypes
import concourse.bass as bass
import concourse.mybir as mybir
from concourse.bass_utils import run_bass_kernel_spmd

F32 = mybir.dt.float32
BF16 = mybir.dt.bfloat16
U32 = mybir.dt.uint32
AF = mybir.ActivationFunctionType
OP = mybir.AluOpType
AXX = mybir.AxisListType.X
BT = bass.broadcast_tensor_aps

NCORES = 8
TOK = 2048
HALO = 3
DM = 1024
DIP = 4240
DI = 2048
NH = 16
HD = 128
NEC = 17
XW = DIP - DI
NTCH = 16
TILE = 512
NTILES = 4
CPT = 4
EXP = 8
FF = 2048
CAP = 5120
NCT = CAP // 128
OOB = 1.0e9
BIGI = 1.0e6
LN_EPS = 1e-5
RMS_EPS = 1e-5
TH = TOK + HALO
KPH = int(_os.environ.get("KPH", "99"))


class _StopEmit(Exception):
    pass


def _ckpt(n):
    if KPH < n:
        raise _StopEmit()


def build_nc():
    nc = bass.Bass("TRN2", target_bir_lowering=False)

    def din(name, shape, dt):
        return nc.dram_tensor(name, shape, dt, kind="ExternalInput").ap()

    u_sl = din("u_sl", [TH, DM], F32)
    ipw_t = din("ipw_t", [DM, DIP], F32)
    outw_t = din("outw_t", [DI, DM], F32)
    convw_d = din("convw", [128, NEC * 4], F32)
    convb_d = din("convb", [128, NEC], F32)
    dtb_d = din("dtb_col", [NH, 1], F32)
    alog_d = din("alog_col", [NH, 1], F32)
    dvec_d = din("dvec", [1, NH], F32)
    rmsw_d = din("rmsw_col", [128, NH], F32)
    wg_d = din("wg", [DM, EXP], F32)
    w_in_d = din("w_in_e", [DM, 2 * FF], F32)
    w_out_d = din("w_out_e", [FF, DM], F32)
    ltmask_d = din("ltmask", [128, 128], F32)
    ltri_d = din("ltri", [128, 128], F32)
    identb_d = din("identb", [128, 128], BF16)
    identf_d = din("identf", [128, 128], F32)
    e16f_d = din("e16f", [16, 16 * 128], F32)
    e16n_d = din("e16n", [16, 16 * 128], F32)
    e64b_d = din("e64b", [16, 16 * 64], BF16)
    i16_d = din("i16", [16, 16], F32)
    ones1f_d = din("ones1f", [1, 128], F32)
    onescol_d = din("onescol", [128, 1], F32)
    o1664_d = din("ones16_64", [16, 64], F32)
    prevmask_d = din("prevmask", [8, 2], F32)
    iota8_d = din("iota8", [128, 8], F32)
    iota8b_d = din("iota8b", [128, 8], F32)
    sgidx_d = din("sgidx", [64, 1], U32)
    szero_d = din("szero", [64, 1], F32)
    ecvec_d = din("ecvec", [128, 1], F32)
    qtr_d = [din(f"qtr{j}", [128, 1], F32) for j in range(4)]
    qpool_d = din("qpool", [128, 1], F32)

    yout = nc.dram_tensor("yout", [TOK, DM], F32, kind="ExternalOutput").ap()
    dbg = nc.dram_tensor("dbg", [128, 4096], F32, kind="ExternalOutput").ap()
    KDBG2 = _os.environ.get("KDBG2", "0") == "1"

    xn_slice = nc.dram_tensor("xn_slice", [TOK, DM], BF16).ap()
    cnt_sl = nc.dram_tensor("cnt_sl", [1, 16], F32).ap()
    d01_sl = nc.dram_tensor("d01_sl", [128, 2 * NTCH], F32).ap()
    sfin_sl = nc.dram_tensor("sfin_sl", [64, NH * HD], F32).ap()
    pool_sl = nc.dram_tensor("pool_sl", [CAP, DM], BF16).ap()
    cnt_all = nc.dram_tensor("cnt_all", [8, 16], F32).ap()
    guard0 = nc.dram_tensor("guard0", [4096, 1024], F32).ap()
    d01_all = nc.dram_tensor("d01_all", [NCORES * 128, 2 * NTCH], F32).ap()
    guard1 = nc.dram_tensor("guard1", [4096, 1024], F32).ap()
    sfin_all = nc.dram_tensor("sfin_all", [NCORES * 64, NH * HD], F32).ap()
    guard2 = nc.dram_tensor("guard2", [4096, 1024], F32).ap()
    xn_all = nc.dram_tensor("xn_all", [NCORES * TOK, DM], BF16).ap()
    guard3 = nc.dram_tensor("guard3", [4096, 1024], F32).ap()
    pool_all = nc.dram_tensor("pool_all", [NCORES * CAP, DM], BF16).ap()
    guard4 = nc.dram_tensor("guard4", [4096, 1024], F32).ap()
    bufd = nc.dram_tensor("bufd", [CAP + 512, DM], BF16).ap()
    wt_sp = nc.dram_tensor("wt_sp", [NTCH, 128, NH * 128], BF16).ap()
    xtm_sp = nc.dram_tensor("xtm_sp", [NTCH, 128, NH * 128], BF16).ap()
    s_sp = nc.dram_tensor("s_sp", [NTCH, 64, NH * HD], BF16).ap()
    uattn = nc.dram_tensor("uattn", [TOK, DM], F32).ap()

    es = contextlib.ExitStack()
    with es:
        E = es.enter_context
        sems = {}
        cnt = {}
        for s in ["sd", "sg", "sp", "sv", "sa", "sq", "sc"]:
            sems[s] = E(nc.semaphore(s))
            cnt[s] = 0

        _selfw = {"sv": "vector", "sa": "scalar", "sq": "gpsimd"}

        def inc(instr, s, n=1, sw=True):
            instr.then_inc(sems[s], n)
            cnt[s] += n
            if sw and s in _selfw:
                getattr(nc, _selfw[s]).wait_ge(sems[s], cnt[s])

        def w(eng, s, val=None):
            v = cnt[s] if val is None else val
            if v > 0:
                eng.wait_ge(sems[s], v)

        def barrier(*engs):
            for e in engs:
                for s in ["sd", "sg", "sp", "sv", "sa", "sq"]:
                    w(e, s)

        TE, VE, SC, GP, SY = nc.tensor, nc.vector, nc.scalar, nc.gpsimd, nc.sync

        def dcap(col, src_ap, width, parts=128):
            if not KDBG2:
                return
            w(VE, "sa")
            w(VE, "sp")
            w(VE, "sd")
            w(VE, "sg")
            w(VE, "sq")
            inc(VE.tensor_copy(dbg_st[0:parts, 0:width], src_ap), "sv")
            w(SY, "sv")
            inc(SY.dma_start(out=dbg[0:parts, col:col + width], in_=dbg_st[0:parts, 0:width]), "sd", 16)
            w(VE, "sd")

        d01_acc = E(nc.sbuf_tensor("d01_acc", [128, 2 * NTCH], F32))
        ltmask = E(nc.sbuf_tensor("c_ltmask", [128, 128], F32))
        ltri = E(nc.sbuf_tensor("c_ltri", [128, 128], F32))
        identb = E(nc.sbuf_tensor("c_identb", [128, 128], BF16))
        identf = E(nc.sbuf_tensor("c_identf", [128, 128], F32))
        e16f = E(nc.sbuf_tensor("c_e16f", [16, 16 * 128], F32))
        e16n = E(nc.sbuf_tensor("c_e16n", [16, 16 * 128], F32))
        e64b = E(nc.sbuf_tensor("c_e64b", [16, 16 * 64], BF16))
        i16 = E(nc.sbuf_tensor("c_i16", [16, 16], F32))
        ones1f = E(nc.sbuf_tensor("c_ones1f", [1, 128], F32))
        onescol = E(nc.sbuf_tensor("c_onescol", [128, 1], F32))
        o1664 = E(nc.sbuf_tensor("c_o1664", [16, 64], F32))
        prevmask = E(nc.sbuf_tensor("c_prevmask", [8, 2], F32))
        iota8 = E(nc.sbuf_tensor("c_iota8", [128, 8], F32))
        iota8b = E(nc.sbuf_tensor("c_iota8b", [128, 8], F32))
        sgidx = E(nc.sbuf_tensor("c_sgidx", [64, 1], U32))
        szero = E(nc.sbuf_tensor("c_szero", [64, 1], F32))
        ecvec = E(nc.sbuf_tensor("c_ecvec", [128, 1], F32))
        qtr = [E(nc.sbuf_tensor(f"c_qtr{j}", [128, 1], F32)) for j in range(4)]
        qpool = E(nc.sbuf_tensor("c_qpool", [128, 1], F32))
        dtb = E(nc.sbuf_tensor("c_dtb", [NH, 1], F32))
        alog = E(nc.sbuf_tensor("c_alog", [NH, 1], F32))
        dvec = E(nc.sbuf_tensor("c_dvec", [1, NH], F32))
        rmsw = E(nc.sbuf_tensor("c_rmsw", [128, NH], F32))
        convw = E(nc.sbuf_tensor("c_convw", [128, NEC * 4], F32))
        convb = E(nc.sbuf_tensor("c_convb", [128, NEC], F32))
        wgb = E(nc.sbuf_tensor("wgb", [128, 8 * 8], BF16))
        expa = E(nc.sbuf_tensor("expa", [NH, 1], F32))
        dbc = E(nc.sbuf_tensor("dbc", [128, NH], F32))
        wdi = E(nc.sbuf_tensor("wdi", [128, NH * 128], BF16))
        logits = E(nc.sbuf_tensor("logits", [128, NTCH * 8], F32))
        cump = E(nc.sbuf_tensor("cump", [16, NTCH * 128], F32))
        lam_bf = E(nc.sbuf_tensor("lam_bf", [16, NTCH * 128], BF16))
        lamg_bf = E(nc.sbuf_tensor("lamg_bf", [16, NTCH * 128], BF16))
        bendx = E(nc.sbuf_tensor("bendx", [16, NTCH + 1], F32))
        gprev = E(nc.sbuf_tensor("gprev", [16, NTCH], F32))
        cfm_all = E(nc.sbuf_tensor("cfm_all", [64, NTCH * 128], BF16))
        oh1a = E(nc.sbuf_tensor("oh1a", [128, NTCH * 8], F32))
        oh2a = E(nc.sbuf_tensor("oh2a", [128, NTCH * 8], F32))
        lp0a = E(nc.sbuf_tensor("lp0a", [128, NTCH * 8], F32))
        lp1a = E(nc.sbuf_tensor("lp1a", [128, NTCH * 8], F32))
        va = E(nc.sbuf_tensor("va", [128, NTCH * 2], F32))
        idxa = E(nc.sbuf_tensor("idxa", [128, NTCH * 2], F32))
        ga = E(nc.sbuf_tensor("ga", [128, NTCH * 2], F32))
        cgi = E(nc.sbuf_tensor("cgi", [128, NTCH * 2], U32))
        base_run = E(nc.sbuf_tensor("base_run", [1, 16], F32))
        gbrow = E(nc.sbuf_tensor("gbrow", [1, 16], F32))
        gb_bc = E(nc.sbuf_tensor("gb_bc", [128, 16], F32))
        sin_bf = E(nc.sbuf_tensor("sin_bf", [64, NH * HD], BF16))
        didx = E(nc.sbuf_tensor("didx", [128, 4 * 128], U32))
        zt = E(nc.sbuf_tensor("zt", [128, DM], BF16))
        rr6 = E(nc.sbuf_tensor("rr6", [128, 8], F32))
        epsb = E(nc.sbuf_tensor("epsb", [128, 1], F32))
        dbg_st = E(nc.sbuf_tensor("dbg_st", [128, 512], F32))
        rc = E(nc.sbuf_tensor("rc", [128, 12], F32))

        if _os.environ.get("KADDR") == "1":
            for _nm, _h in [("rc", rc), ("d01_acc", d01_acc), ("cgi", cgi), ("ga", ga),
                            ("dbg_st", dbg_st), ("epsb", epsb), ("zt", zt), ("didx", didx),
                            ("scratch", nc.dma_scratch), ("ltmask", ltmask)]:
                try:
                    _ml = nc.lookup_mloc(_h)
                    print("ADDR", _nm, _ml.addr, getattr(_ml, "size_bytes", None))
                except Exception as _e:
                    print("ADDR", _nm, "err", _e)

        p_a = E(nc.psum_tensor("p_a", [128, 4 * 128], BF16))
        p_b = E(nc.psum_tensor("p_b", [128, 512], F32))
        p_c = E(nc.psum_tensor("p_c", [128, 512], F32))
        p_m = E(nc.psum_tensor("p_m", [128, 128], F32))
        p_g = E(nc.psum_tensor("p_g", [128, 128], F32))
        p_y = E(nc.psum_tensor("p_y", [128, 512], F32))
        p_ds = E(nc.psum_tensor("p_ds", [64, 512], F32))
        p_s = E(nc.psum_tensor("p_s", [128, 512], F32))

        _open = []

        def _mkstack():
            s = contextlib.ExitStack()
            _open.append(s)
            return s

        def _close(s):
            _open.remove(s)
            s.close()

        stAB = _mkstack()
        xn_fm = stAB.enter_context(nc.sbuf_tensor("xn_fm", [128, 8 * TH], BF16))
        stW = _mkstack()
        ipw_bf = stW.enter_context(nc.sbuf_tensor("ipw_bf", [128, 8 * XW], BF16))

        with nc.Block() as block:
            try:
                stA = _mkstack()
                EA = stA.enter_context
                wstage = EA(nc.sbuf_tensor("wstage", [128, XW], F32))
                u_tm = EA(nc.sbuf_tensor("u_tm", [128, DM], F32))
                sq_tm = EA(nc.sbuf_tensor("sq_tm", [128, DM], BF16))
                xn_tm = EA(nc.sbuf_tensor("xn_tm", [128, DM], BF16))
                lnst = EA(nc.sbuf_tensor("lnst", [128, 4], F32))
                cntsb = EA(nc.sbuf_tensor("cntsb", [8, 16], F32))

                for dst, src in [
                    (ltmask, ltmask_d), (ltri, ltri_d), (identb, identb_d),
                    (identf, identf_d), (e16f, e16f_d), (e16n, e16n_d),
                    (e64b, e64b_d), (i16, i16_d), (ones1f, ones1f_d),
                    (onescol, onescol_d), (o1664, o1664_d), (prevmask, prevmask_d),
                    (iota8, iota8_d), (iota8b, iota8b_d), (sgidx, sgidx_d),
                    (szero, szero_d), (ecvec, ecvec_d), (qtr[0], qtr_d[0]),
                    (qtr[1], qtr_d[1]), (qtr[2], qtr_d[2]), (qtr[3], qtr_d[3]),
                    (qpool, qpool_d), (dtb, dtb_d),
                    (alog, alog_d), (dvec, dvec_d), (rmsw, rmsw_d),
                    (convw, convw_d), (convb, convb_d),
                ]:
                    inc(SY.dma_start(out=dst[:, :], in_=src), "sd", 16)

                w(SC, "sd")
                inc(SC.activation(out=expa[:, :], in_=alog[:, :], func=AF.Exp), "sa")
                inc(GP.memset(zt[:, :], 0.0), "sq")
                inc(GP.memset(epsb[:, :], LN_EPS), "sq")
                inc(GP.memset(base_run[:, :], 0.0), "sq")
                inc(GP.memset(bendx[:, 0:1], 0.0), "sq")
                inc(GP.memset(d01_acc[:, :], 7.0), "sq")
                w(GP, "sq")
                for i in range(NCT):
                    inc(GP.dma_start(out=bufd[i * 128:(i + 1) * 128, :], in_=zt[:, :]), "sg", 16)

                _ckpt(5)
                w(TE, "sd")
                inc(TE.matmul(p_m[:, 0:16], ones1f[:, :], dvec[:, :], start=True, stop=True), "sp")
                w(VE, "sp")
                inc(VE.tensor_copy(dbc[:, :], p_m[:, 0:16]), "sv")
                for h in range(NH):
                    inc(VE.tensor_scalar(wdi[:, h * 128:(h + 1) * 128], identb[:, :],
                                         dbc[:, h:h + 1], None, op0=OP.mult), "sv")

                _ckpt(7)
                for d in range(8):
                    w(SY, "sa")
                    inc(SY.dma_start(out=wstage[:, :],
                                     in_=ipw_t.rearrange("(a p) e -> p a e", p=128)[:, d, DI:DIP]), "sd", 16)
                    w(SC, "sd")
                    inc(SC.activation(out=ipw_bf[:, d * XW:(d + 1) * XW], in_=wstage[:, :], func=AF.Copy), "sa")
                w(SY, "sa")
                inc(SY.dma_start(out=wstage[:, 0:64],
                                 in_=wg_d.rearrange("(a p) e -> p a e", p=128)), "sd", 16)
                w(SC, "sd")
                inc(SC.activation(out=wgb[:, :], in_=wstage[:, 0:64], func=AF.Copy), "sa")

                _ckpt(9)
                for it in range(17):
                    rows = u_sl[0:128, :] if it == 0 else \
                        u_sl[HALO + (it - 1) * 128: HALO + it * 128, :]
                    w(SY, "sp")
                    w(SY, "sv")
                    inc(SY.dma_start(out=u_tm[:, :], in_=rows), "sd", 16)
                    w(VE, "sd")
                    inc(VE.reduce_sum(lnst[:, 0:1], u_tm[:, :], axis=AXX), "sv")
                    w(SC, "sd")
                    inc(SC.activation(out=sq_tm[:, :], in_=u_tm[:, :], func=AF.Square), "sa")
                    w(VE, "sa")
                    inc(VE.reduce_sum(lnst[:, 1:2], sq_tm[:, :], axis=AXX), "sv")
                    inc(VE.tensor_scalar(lnst[:, 2:3], lnst[:, 0:1], lnst[:, 0:1], 1.0 / (DM * DM),
                                         op0=OP.mult, op1=OP.mult), "sv")
                    inc(VE.scalar_tensor_tensor(lnst[:, 3:4], lnst[:, 1:2], 1.0 / DM,
                                                lnst[:, 2:3], op0=OP.mult, op1=OP.subtract), "sv")
                    w(SC, "sv")
                    inc(SC.activation(out=lnst[:, 2:3], in_=lnst[:, 3:4], func=AF.Sqrt, bias=epsb[:, 0:1]), "sa")
                    w(VE, "sa")
                    inc(VE.reciprocal(lnst[:, 3:4], lnst[:, 2:3]), "sv")
                    inc(VE.tensor_scalar(lnst[:, 0:1], lnst[:, 0:1], 1.0 / DM, None, op0=OP.mult), "sv")
                    inc(VE.tensor_scalar(xn_tm[:, :], u_tm[:, :], lnst[:, 0:1], lnst[:, 3:4],
                                         op0=OP.subtract, op1=OP.mult), "sv")
                    w(TE, "sv")
                    for half in range(2):
                        for d in range(4):
                            dd = half * 4 + d
                            instr = TE.transpose(p_a[:, d * 128:(d + 1) * 128],
                                                 xn_tm[:, dd * 128:(dd + 1) * 128], identb[:, :])
                        inc(instr, "sp")
                        w(VE, "sp")
                        for d in range(4):
                            dd = half * 4 + d
                            if it == 0:
                                dst = xn_fm[:, dd * TH + 0: dd * TH + 3]
                                srcp = p_a[:, d * 128 + 0: d * 128 + 3]
                            else:
                                k = it - 1
                                dst = xn_fm[:, dd * TH + HALO + k * 128: dd * TH + HALO + (k + 1) * 128]
                                srcp = p_a[:, d * 128:(d + 1) * 128]
                            inc(VE.tensor_copy(dst, srcp), "sv")
                        if half == 0:
                            w(TE, "sv")
                    if it > 0:
                        k = it - 1
                        w(SY, "sv")
                        inc(SY.dma_start(out=xn_slice[k * 128:(k + 1) * 128, :], in_=xn_tm[:, :]), "sd", 16)

                _ckpt(11)
                for k in range(NTCH):
                    w(TE, "sa")
                    w(TE, "sv")
                    for d in range(8):
                        instr = TE.matmul(p_m[:, 0:8],
                                          xn_fm[:, d * TH + HALO + k * 128: d * TH + HALO + (k + 1) * 128],
                                          wgb[:, d * 8:(d + 1) * 8],
                                          start=(d == 0), stop=(d == 7))
                    inc(instr, "sp")
                    w(VE, "sp")
                    inc(VE.tensor_copy(logits[:, k * 8:(k + 1) * 8], p_m[:, 0:8]), "sv")

                _ckpt(12)
                dcap(0, xn_fm[:, 0 * TH + 3: 0 * TH + 3 + 512], 512)
                for k in range(NTCH):
                    lg = logits[:, k * 8:(k + 1) * 8]
                    w(VE, "sv")
                    inc(VE.reduce_max(rc[:, 0:1], lg, axis=AXX, negate=True), "sv")
                    w(SC, "sv")
                    inc(SC.activation(out=rr6[:, :], in_=lg, func=AF.Exp, bias=rc[:, 0:1]), "sa")
                    w(VE, "sa")
                    inc(VE.reduce_sum(rc[:, 1:2], rr6[:, :], axis=AXX), "sv")
                    inc(VE.reciprocal(rc[:, 2:3], rc[:, 1:2]), "sv")
                    inc(VE.tensor_scalar(rr6[:, :], rr6[:, :], rc[:, 2:3], None, op0=OP.mult), "sv")
                    inc(VE.reduce_max(va[:, k * 2:k * 2 + 1], rr6[:, :], axis=AXX), "sv")
                    inc(VE.tensor_scalar(oh1a[:, k * 8:(k + 1) * 8], rr6[:, :],
                                         va[:, k * 2:k * 2 + 1], None, op0=OP.is_equal), "sv")
                    inc(VE.scalar_tensor_tensor(rc[:, 4:12], oh1a[:, k * 8:(k + 1) * 8], -BIGI,
                                                iota8b[:, :], op0=OP.mult, op1=OP.add), "sv")
                    inc(VE.tensor_reduce(idxa[:, k * 2:k * 2 + 1], rc[:, 4:12], axis=AXX, op=OP.min), "sv")
                    inc(VE.tensor_scalar(oh1a[:, k * 8:(k + 1) * 8], iota8[:, :],
                                         idxa[:, k * 2:k * 2 + 1], None, op0=OP.is_equal), "sv")
                    inc(VE.scalar_tensor_tensor(rr6[:, :], oh1a[:, k * 8:(k + 1) * 8], -BIGI,
                                                rr6[:, :], op0=OP.mult, op1=OP.add), "sv")
                    inc(VE.reduce_max(va[:, k * 2 + 1:k * 2 + 2], rr6[:, :], axis=AXX), "sv")
                    inc(VE.tensor_scalar(oh2a[:, k * 8:(k + 1) * 8], rr6[:, :],
                                         va[:, k * 2 + 1:k * 2 + 2], None, op0=OP.is_equal), "sv")
                    inc(VE.scalar_tensor_tensor(rc[:, 4:12], oh2a[:, k * 8:(k + 1) * 8], -BIGI,
                                                iota8b[:, :], op0=OP.mult, op1=OP.add), "sv")
                    inc(VE.tensor_reduce(idxa[:, k * 2 + 1:k * 2 + 2], rc[:, 4:12], axis=AXX, op=OP.min), "sv")
                    inc(VE.tensor_scalar(oh2a[:, k * 8:(k + 1) * 8], iota8[:, :],
                                         idxa[:, k * 2 + 1:k * 2 + 2], None, op0=OP.is_equal), "sv")
                    for kk, oh, lp in ((0, oh1a, lp0a), (1, oh2a, lp1a)):
                        w(TE, "sv")
                        TE.matmul(p_m[:, 0:8], ltri[:, :], oh[:, k * 8:(k + 1) * 8],
                                  start=True, stop=False)
                        instr = TE.matmul(p_m[:, 0:8], ones1f[:, :],
                                          base_run[:, kk * 8:(kk + 1) * 8], start=False, stop=True)
                        inc(instr, "sp")
                        instr = TE.matmul(p_s[0:1, 0:8], onescol[:, :], oh[:, k * 8:(k + 1) * 8],
                                          start=True, stop=True)
                        inc(instr, "sp")
                        w(VE, "sp")
                        inc(VE.tensor_copy(lp[:, k * 8:(k + 1) * 8], p_m[:, 0:8]), "sv")
                        inc(VE.tensor_tensor(base_run[:, kk * 8:(kk + 1) * 8],
                                             base_run[:, kk * 8:(kk + 1) * 8],
                                             p_s[0:1, 0:8], op=OP.add), "sv")
                        w(TE, "sv")

                _ckpt(13)
                w(SY, "sv")
                inc(SY.dma_start(out=cnt_sl[:, :], in_=base_run[:, :]), "sd", 16)
                w(GP, "sd")
                inc(GP.collective_compute("AllGather", OP.bypass,
                                          replica_groups=[list(range(NCORES))],
                                          ins=[cnt_sl.opt()], outs=[cnt_all.opt()]), "sc")
                AG2_C = cnt["sc"]
                inc(GP.collective_compute("AllGather", OP.bypass,
                                          replica_groups=[list(range(NCORES))],
                                          ins=[xn_slice.opt()], outs=[xn_all.opt()]), "sc")
                AG1_C = cnt["sc"]

                _ckpt(14)
                w(SY, "sc", AG2_C)
                inc(SY.dma_start(out=cntsb[:, :], in_=cnt_all), "sd", 16)
                w(TE, "sd")
                instr = TE.matmul(p_s[0:1, 0:16], prevmask[:, 0:1], cntsb[:, :], start=True, stop=True)
                inc(instr, "sp")
                instr = TE.matmul(p_s[0:1, 16:32], prevmask[:, 1:2], cntsb[:, :], start=True, stop=True)
                inc(instr, "sp")
                w(VE, "sp")
                inc(VE.tensor_copy(gbrow[:, 0:8], p_s[0:1, 0:8]), "sv")
                inc(VE.tensor_copy(gbrow[:, 8:16], p_s[0:1, 16:24]), "sv")
                inc(VE.tensor_tensor(gbrow[:, 8:16], gbrow[:, 8:16], p_s[0:1, 8:16], op=OP.add), "sv")
                w(TE, "sv")
                instr = TE.matmul(p_s[:, 0:16], ones1f[:, :], gbrow[:, :], start=True, stop=True)
                inc(instr, "sp")
                w(VE, "sp")
                inc(VE.tensor_copy(gb_bc[:, :], p_s[:, 0:16]), "sv")

                for k in range(NTCH):
                    sl = slice(k * 8, (k + 1) * 8)
                    for kk, oh, lp in ((0, oh1a, lp0a), (1, oh2a, lp1a)):
                        inc(VE.tensor_tensor(rr6[:, :], lp[:, sl],
                                             gb_bc[:, kk * 8:(kk + 1) * 8], op=OP.add), "sv")
                        inc(VE.tensor_tensor(rr6[:, :], rr6[:, :], oh[:, sl], op=OP.mult), "sv")
                        inc(VE.reduce_sum(rc[:, 0:1], rr6[:, :], axis=AXX), "sv")
                        inc(VE.tensor_scalar(rc[:, 1:2], rc[:, 0:1], float(CAP), None,
                                             op0=OP.is_le), "sv")
                        if kk == 0:
                            inc(VE.tensor_copy(rc[:, 10:11], rc[:, 1:2]), "sv")
                        inc(VE.tensor_scalar(rc[:, 2:3], rc[:, 0:1], -1.0, None, op0=OP.add), "sv")
                        inc(VE.scalar_tensor_tensor(rc[:, 3:4], idxa[:, k * 2 + kk:k * 2 + kk + 1],
                                                    float(CAP), rc[:, 2:3], op0=OP.mult, op1=OP.add), "sv")
                        inc(VE.tensor_scalar(rc[:, 4:5], rc[:, 1:2], -OOB, OOB,
                                             op0=OP.mult, op1=OP.add), "sv")
                        inc(VE.scalar_tensor_tensor(rc[:, 5 + kk:6 + kk], rc[:, 3:4], rc[:, 1:2],
                                                    rc[:, 4:5], op0=OP.mult, op1=OP.add), "sv")
                        inc(VE.tensor_scalar(ga[:, k * 2 + kk:k * 2 + kk + 1],
                                             va[:, k * 2 + kk:k * 2 + kk + 1],
                                             rc[:, 1:2], None, op0=OP.mult), "sv")
                    inc(VE.tensor_tensor(rc[:, 7:8], ga[:, k * 2:k * 2 + 1],
                                         ga[:, k * 2 + 1:k * 2 + 2], op=OP.add), "sv")
                    inc(VE.tensor_scalar(rc[:, 7:8], rc[:, 7:8], 1e-9, None, op0=OP.max), "sv")
                    inc(VE.reciprocal(rc[:, 8:9], rc[:, 7:8]), "sv")
                    inc(VE.tensor_scalar(ga[:, k * 2:k * 2 + 1], ga[:, k * 2:k * 2 + 1],
                                         rc[:, 8:9], None, op0=OP.mult), "sv")
                    inc(VE.tensor_scalar(ga[:, k * 2 + 1:k * 2 + 2], ga[:, k * 2 + 1:k * 2 + 2],
                                         rc[:, 8:9], None, op0=OP.mult), "sv")
                    for kk2 in range(2):
                        inc(VE.tensor_scalar(rc[:, 9:10], rc[:, 1:2] if kk2 == 1 else rc[:, 10:11],
                                             -1.0, 1.0, op0=OP.mult, op1=OP.add), "sv")
                        inc(VE.tensor_scalar(rc[:, 9:10], rc[:, 9:10], qpool[:, 0:1], None,
                                             op0=OP.mult), "sv")
                        inc(VE.scalar_tensor_tensor(rc[:, 11:12], rc[:, 5 + kk2:6 + kk2],
                                                    rc[:, 1:2] if kk2 == 1 else rc[:, 10:11],
                                                    rc[:, 9:10], op0=OP.mult, op1=OP.add), "sv")
                        inc(VE.tensor_copy(cgi[:, kk2 * NTCH + k:kk2 * NTCH + k + 1], rc[:, 11:12]), "sv")
                    inc(VE.tensor_copy(d01_acc[:, 2 * k:2 * k + 1], rc[:, 5:6]), "sv")
                    inc(VE.tensor_copy(d01_acc[:, 2 * k + 1:2 * k + 2], rc[:, 6:7]), "sv")

                dcap(0, ga[:, :], 32)
                dcap(32, idxa[:, :], 32)
                dcap(64, va[:, :], 32)
                dcap(96, gb_bc[:, :], 16)
                dcap(112, lp0a[:, 0:8], 8)
                dcap(120, oh1a[:, 0:8], 8)
                dcap(128, cgi[:, :], 32)
                dcap(160, logits[:, 0:8], 8)
                dcap(3616, d01_acc[:, :], 32)
                _ckpt(15)
                w(SY, "sv")
                inc(SY.dma_start(out=d01_sl, in_=d01_acc[:, :]), "sd", 16)
                w(GP, "sd")
                inc(GP.collective_compute("AllGather", OP.bypass,
                                          replica_groups=[list(range(NCORES))],
                                          ins=[d01_sl.opt()], outs=[d01_all.opt()]), "sc")
                AG3_C = cnt["sc"]

                barrier(TE, VE, SC, GP, SY)
                _close(stA)

                _ckpt(20)
                st1 = _mkstack()
                E1 = st1.enter_context
                xbc_in = E1(nc.sbuf_tensor("xbc_in", [128, NEC * (TILE + HALO)], BF16))
                xbc_out = E1(nc.sbuf_tensor("xbc_out", [128, NEC * TILE], BF16))
                conv_sa = E1(nc.sbuf_tensor("conv_sa", [128, TILE], F32))
                conv_sb = E1(nc.sbuf_tensor("conv_sb", [128, TILE], F32))
                dtp = E1(nc.sbuf_tensor("dtp", [16, TILE], F32))
                wrtp = E1(nc.sbuf_tensor("wrtp", [128, 128], F32))
                wtmc = E1(nc.sbuf_tensor("wtmc", [128, 16], F32))
                btm = E1(nc.sbuf_tensor("btm", [128, 128], BF16))
                bw_tm = E1(nc.sbuf_tensor("bw_tm", [128, 64], BF16))
                msb = E1(nc.sbuf_tensor("msb", [128, 128], F32))
                mex = E1(nc.sbuf_tensor("mex", [128, 128], F32))
                mlhs = E1(nc.sbuf_tensor("mlhs", [16, 128], F32))
                wt_t = E1(nc.sbuf_tensor("wt_t", [128, NH * 128], BF16))
                xtm_t = E1(nc.sbuf_tensor("xtm_t", [128, NH * 128], BF16))
                sA = E1(nc.sbuf_tensor("sA", [64, NH * HD], F32))
                sB = E1(nc.sbuf_tensor("sB", [64, NH * HD], F32))
                s_tmp = E1(nc.sbuf_tensor("s_tmp", [64, 512], F32))
                s_spb = E1(nc.sbuf_tensor("s_spb", [64, NH * HD], BF16))
                g64 = E1(nc.sbuf_tensor("g64", [64, 16], F32))
                gcol = E1(nc.sbuf_tensor("gcol", [16, 2], F32))
                dg16 = E1(nc.sbuf_tensor("dg16", [16, 16], F32))
                s_cur, s_nxt = sA, sB

                inc(GP.memset(sA[:, :], 0.0), "sq")
                inc(GP.memset(wrtp[:, :], 0.0), "sq")
                barrier(TE, VE, SC, GP, SY)

                for kt in range(NTILES):
                    t0 = kt * TILE
                    if kt == 0:
                        for e in range(NEC):
                            pp = p_b if e % 2 == 0 else p_c
                            w(TE, "sv")
                            for d in range(8):
                                instr = TE.matmul(pp[:, 0:HALO],
                                                  ipw_bf[:, d * XW + e * 128: d * XW + (e + 1) * 128],
                                                  xn_fm[:, d * TH + 0: d * TH + HALO],
                                                  start=(d == 0), stop=(d == 7))
                            inc(instr, "sp")
                            w(VE, "sp")
                            inc(VE.tensor_copy(
                                xbc_in[:, e * (TILE + HALO): e * (TILE + HALO) + HALO], pp[:, 0:HALO]), "sv")
                    else:
                        xv = xbc_in[:, :].rearrange("p (e t) -> p e t", t=TILE + HALO)
                        inc(VE.tensor_copy(xv[:, :, 0:HALO], xv[:, :, TILE:TILE + HALO]), "sv")

                    for e in range(NEC):
                        pp = p_b if e % 2 == 0 else p_c
                        w(TE, "sv")
                        for d in range(8):
                            instr = TE.matmul(pp[:, :],
                                              ipw_bf[:, d * XW + e * 128: d * XW + (e + 1) * 128],
                                              xn_fm[:, d * TH + HALO + t0: d * TH + HALO + t0 + TILE],
                                              start=(d == 0), stop=(d == 7))
                        inc(instr, "sp")
                        w(VE, "sp")
                        inc(VE.tensor_copy(
                            xbc_in[:, e * (TILE + HALO) + HALO: (e + 1) * (TILE + HALO)], pp[:, :]), "sv", sw=False)

                    w(TE, "sv")
                    for d in range(8):
                        instr = TE.matmul(p_s[0:16, 0:TILE],
                                          ipw_bf[:, d * XW + NEC * 128: d * XW + NEC * 128 + 16],
                                          xn_fm[:, d * TH + HALO + t0: d * TH + HALO + t0 + TILE],
                                          start=(d == 0), stop=(d == 7))
                    inc(instr, "sp")
                    w(SC, "sp")
                    inc(SC.activation(out=dtp[:, :], in_=p_s[0:16, 0:TILE], func=AF.Abs,
                                      bias=dtb[:, 0:1]), "sa")
                    inc(SC.activation(out=dtp[:, :], in_=dtp[:, :], func=AF.Exp, scale=-1.0), "sa")
                    inc(SC.activation(out=dtp[:, :], in_=dtp[:, :], func=AF.Ln, bias=1.0), "sa")
                    inc(SC.activation(out=conv_sa[0:16, :], in_=p_s[0:16, 0:TILE], func=AF.Relu,
                                      bias=dtb[:, 0:1]), "sa")
                    w(VE, "sa")
                    inc(VE.tensor_tensor(dtp[:, :], dtp[:, :], conv_sa[0:16, :], op=OP.add), "sv")
                    inc(VE.tensor_scalar(dtp[:, :], dtp[:, :], expa[:, 0:1], None, op0=OP.mult), "sv")
                    for j in range(CPT):
                        c = kt * CPT + j
                        inc(VE.tensor_tensor_scan(cump[:, c * 128:(c + 1) * 128],
                                                  dtp[:, j * 128:(j + 1) * 128],
                                                  dtp[:, j * 128:(j + 1) * 128],
                                                  0.0, op0=OP.add, op1=OP.bypass), "sv")
                        w(SC, "sv")
                        inc(SC.activation(out=lam_bf[:, c * 128:(c + 1) * 128],
                                          in_=cump[:, c * 128:(c + 1) * 128], func=AF.Exp,
                                          scale=-1.0), "sa")
                        w(VE, "sa")
                        inc(VE.tensor_copy(bendx[:, c + 1:c + 2],
                                           cump[:, c * 128 + 127:c * 128 + 128]), "sv")

                    for e in range(NEC):
                        base = e * (TILE + HALO)
                        w(VE, "sv")
                        w(VE, "sa")
                        inc(VE.tensor_scalar(conv_sa[:, :], xbc_in[:, base + 0: base + TILE],
                                             convw[:, e * 4 + 0:e * 4 + 1], None, op0=OP.mult), "sv", sw=False)
                        inc(VE.scalar_tensor_tensor(conv_sb[:, :], xbc_in[:, base + 1: base + 1 + TILE],
                                                    convw[:, e * 4 + 1:e * 4 + 2], conv_sa[:, :],
                                                    op0=OP.mult, op1=OP.add), "sv", sw=False)
                        inc(VE.scalar_tensor_tensor(conv_sa[:, :], xbc_in[:, base + 2: base + 2 + TILE],
                                                    convw[:, e * 4 + 2:e * 4 + 3], conv_sb[:, :],
                                                    op0=OP.mult, op1=OP.add), "sv", sw=False)
                        inc(VE.scalar_tensor_tensor(conv_sb[:, :], xbc_in[:, base + 3: base + 3 + TILE],
                                                    convw[:, e * 4 + 3:e * 4 + 4], conv_sa[:, :],
                                                    op0=OP.mult, op1=OP.add), "sv", sw=False)
                        w(SC, "sv")
                        inc(SC.activation(out=xbc_out[:, e * TILE:(e + 1) * TILE], in_=conv_sb[:, :],
                                          func=AF.Silu, bias=convb[:, e:e + 1]), "sa")

                    if False:
                        pass
                    for j in range(CPT):
                        c = kt * CPT + j
                        cslice = slice(c * 128, (c + 1) * 128)
                        bc_sl = slice(16 * TILE + j * 128, 16 * TILE + (j + 1) * 128)
                        w(SC, "sv")
                        inc(SC.activation(out=gcol[:, 0:1], in_=cump[:, c * 128 + 127:c * 128 + 128],
                                          func=AF.Exp, scale=-1.0), "sa")
                        inc(SC.activation(out=gcol[:, 1:2], in_=cump[:, c * 128 + 127:c * 128 + 128],
                                          func=AF.Copy, scale=-1.0), "sa")
                        inc(SC.activation(out=wrtp[0:16, :], in_=cump[:, cslice],
                                          func=AF.Exp, bias=gcol[:, 1:2]), "sa")
                        w(VE, "sa")
                        inc(VE.tensor_scalar(dg16[:, :], i16[:, :], gcol[:, 0:1], None, op0=OP.mult), "sv")
                        w(TE, "sv")
                        w(TE, "sa")
                        instr = TE.matmul(p_s[0:64, 128:144], o1664[:, :], dg16[:, :],
                                          start=True, stop=True)
                        inc(instr, "sp")
                        instr = TE.transpose(p_s[:, 144:272], wrtp[:, :], identf[:, :])
                        inc(instr, "sp")
                        instr = TE.transpose(p_a[:, 0:128], xbc_out[:, bc_sl], identb[:, :])
                        inc(instr, "sp")
                        w(VE, "sp")
                        inc(VE.tensor_copy(g64[:, :], p_s[0:64, 128:144]), "sv")
                        inc(VE.tensor_copy(wtmc[:, :], p_s[:, 144:272][:, 0:16]), "sv")
                        inc(VE.tensor_copy(btm[:, :], p_a[:, 0:128]), "sv")
                        inc(VE.tensor_copy(cfm_all[:, cslice], xbc_out[64:128, bc_sl]), "sv")

                        for h in range(NH):
                            hs = slice(h * 128, (h + 1) * 128)
                            w(TE, "sv")
                            instr = TE.transpose(p_a[:, 128:256],
                                                 xbc_out[:, h * TILE + j * 128: h * TILE + (j + 1) * 128],
                                                 identb[:, :])
                            inc(instr, "sp")
                            w(VE, "sp")
                            inc(VE.tensor_copy(xtm_t[:, hs], p_a[:, 128:256]), "sv", sw=False)
                            inc(VE.tensor_scalar(mlhs[:, :], cump[:, cslice],
                                                 e16f[:, h * 128:h * 128 + 1], None, op0=OP.mult), "sv")
                            w(TE, "sv")
                            TE.matmul(p_m[:, :], mlhs[:, :], e16f[:, hs], start=True, stop=False)
                            instr = TE.matmul(p_m[:, :], e16n[:, hs], cump[:, cslice],
                                              start=False, stop=True)
                            inc(instr, "sp")
                            TE.matmul(p_g[:, :], xbc_out[0:64, bc_sl], cfm_all[:, cslice],
                                      start=True, stop=False)
                            instr = TE.matmul(p_g[:, :], wdi[:, hs], identb[:, :],
                                              start=False, stop=True)
                            inc(instr, "sp")
                            w(VE, "sp")
                            inc(VE.tensor_tensor(msb[:, :], p_m[:, :], ltmask[:, :], op=OP.add), "sv", sw=False)
                            w(SC, "sv")
                            inc(SC.activation(out=mex[:, :], in_=msb[:, :], func=AF.Exp), "sa")
                            w(VE, "sa")
                            inc(VE.tensor_tensor(wt_t[:, hs], mex[:, :], p_g[:, :], op=OP.mult), "sv", sw=False)
                            inc(VE.tensor_scalar(bw_tm[:, :], btm[:, 0:64], wtmc[:, h:h + 1],
                                                 None, op0=OP.mult), "sv", sw=False)
                            w(TE, "sv")
                            grp = h % 4
                            instr = TE.matmul(p_ds[:, grp * 128:(grp + 1) * 128], bw_tm[:, :],
                                              xtm_t[:, hs], start=True, stop=True)
                            inc(instr, "sp")
                            if grp == 3:
                                g = h // 4
                                gsl = slice(g * 512, (g + 1) * 512)
                                if g == 0:
                                    w(VE, "sd")
                                    inc(VE.tensor_copy(s_spb[:, :], s_cur[:, :]), "sv")
                                    w(SY, "sv")
                                    inc(SY.dma_start(out=s_sp[c], in_=s_spb[:, :]), "sd", 16)
                                sv_view = s_cur[:, gsl].rearrange("p (a b) -> p a b", b=128)
                                gv_view = g64[:, g * 4:(g + 1) * 4].rearrange("p (a o) -> p a o", o=1)
                                aa, bb = BT(sv_view, gv_view)
                                w(VE, "sd")
                                inc(VE.tensor_tensor(s_tmp[:, :].rearrange("p (a b) -> p a b", b=128),
                                                     aa, bb, op=OP.mult), "sv")
                                w(VE, "sp")
                                inc(VE.tensor_tensor(s_nxt[:, gsl], s_tmp[:, :], p_ds[:, :], op=OP.add), "sv")
                                w(TE, "sv")

                        if False:
                            pass
                        w(SY, "sv")
                        inc(SY.dma_start(out=wt_sp[c], in_=wt_t[:, :]), "sd", 16)
                        inc(SY.dma_start(out=xtm_sp[c], in_=xtm_t[:, :]), "sd", 16)
                        s_cur, s_nxt = s_nxt, s_cur
                        w(VE, "sd")
                        w(TE, "sd")

                w(SY, "sv")
                inc(SY.dma_start(out=sfin_sl, in_=s_cur[:, :]), "sd", 16)
                w(GP, "sd")
                inc(GP.collective_compute("AllGather", OP.bypass,
                                          replica_groups=[list(range(NCORES))],
                                          ins=[sfin_sl.opt()], outs=[sfin_all.opt()]), "sc")
                AGS_C = cnt["sc"]
                w(GP, "sc", AGS_C)
                w(GP, "sv")
                inc(GP.indirect_dma_start(out=s_nxt[:, :], out_offset=None, in_=sfin_all,
                                          in_offset=bass.IndirectOffsetOnAxis(ap=sgidx[:, 0:1], axis=0)), "sg", 16)
                w(VE, "sg")
                inc(VE.tensor_scalar(sin_bf[:, :], s_nxt[:, :], szero[:, 0:1], None, op0=OP.mult), "sv")

                inc(VE.tensor_tensor_scan(gprev[:, :], bendx[:, 0:NTCH], bendx[:, 0:NTCH],
                                          0.0, op0=OP.add, op1=OP.bypass), "sv")
                w(SC, "sv")
                inc(SC.activation(out=gprev[:, :], in_=gprev[:, :], func=AF.Exp, scale=-1.0), "sa")
                w(VE, "sa")
                for c in range(NTCH):
                    inc(VE.tensor_scalar(lamg_bf[:, c * 128:(c + 1) * 128],
                                         lam_bf[:, c * 128:(c + 1) * 128],
                                         gprev[:, c:c + 1], None, op0=OP.mult), "sv")

                barrier(TE, VE, SC, GP, SY)
                _close(st1)
                _close(stW)

                _ckpt(30)
                st2 = _mkstack()
                E2 = st2.enter_context
                xrow0 = E2(nc.sbuf_tensor("xrow0", [128, DM], BF16))
                xrow1 = E2(nc.sbuf_tensor("xrow1", [128, DM], BF16))
                dall = E2(nc.sbuf_tensor("dall", [128, 128 * 2], F32))
                dsc = E2(nc.sbuf_tensor("dsc", [128, 128 * 4], F32))

                w(SY, "sc", AG3_C)
                inc(SY.dma_start(out=dall[:, :].rearrange("p (r m) -> p r m", m=2 * NTCH),
                                 in_=d01_all.rearrange("(r p) m -> p r m", p=128)), "sd", 16)
                for kk in range(2):
                    dk = dall[:, :].rearrange("p (n c) -> p n c", c=2)[:, :, kk]
                    t_ = dsc[:, 0:128]
                    v1_ = dsc[:, 128:256]
                    f_ = dsc[:, 256:384]
                    g_ = dsc[:, 384:512]
                    w(VE, "sd")
                    inc(VE.tensor_scalar(t_, dk, ecvec[:, 0:1], None, op0=OP.subtract), "sv")
                    inc(VE.tensor_scalar(v1_, t_, 0.0, None, op0=OP.is_ge), "sv")
                    inc(VE.tensor_scalar(f_, t_, float(CAP), None, op0=OP.is_lt), "sv")
                    inc(VE.tensor_tensor(v1_, v1_, f_, op=OP.mult), "sv")
                    inc(VE.tensor_tensor(t_, t_, v1_, op=OP.mult), "sv")
                    inc(VE.tensor_scalar(f_, v1_, -1.0, 1.0, op0=OP.mult, op1=OP.add), "sv")
                    for par in range(2):
                        qc = qtr[kk * 2 + par]
                        inc(VE.tensor_scalar(g_, f_, qc[:, 0:1], None, op0=OP.mult), "sv")
                        inc(VE.tensor_tensor(didx[:, (kk * 2 + par) * 128:(kk * 2 + par + 1) * 128],
                                             t_, g_, op=OP.add), "sv")

                dcap(3520, d01_acc[:, 0:32], 32)
                w(SY, "sv")
                w(SY, "sc")
                inc(SY.dma_start(out=dsc[:, 0:32], in_=d01_sl), "sd", 16)
                inc(SY.dma_start(out=dsc[:, 32:64], in_=d01_all[0:128, :]), "sd", 16)
                dcap(3552, dsc[:, 0:64], 64)
                dcap(3456, didx[:, 0:16], 16)
                dcap(3472, didx[:, 256:272], 16)
                dcap(3488, dall[:, 0:32], 32)
                w(GP, "sv")
                w(GP, "sc", AG1_C)
                w(GP, "sg")
                SG_D0 = cnt["sg"]
                ld_mark = {}
                sc_mark = {}
                for i in range(128):
                    xr = xrow0 if i % 2 == 0 else xrow1
                    if i >= 2:
                        w(SY, "sg", sc_mark[i - 2])
                    inc(SY.dma_start(out=xr[:, :], in_=xn_all[i * 128:(i + 1) * 128, :]), "sd", 16)
                    ld_mark[i] = cnt["sd"]
                    w(GP, "sd", ld_mark[i])
                    pl0 = (0 * 2 + i % 2) * 128
                    pl1 = (1 * 2 + i % 2) * 128
                    inc(GP.indirect_dma_start(
                        out=bufd, out_offset=bass.IndirectOffsetOnAxis(ap=didx[:, pl0 + i:pl0 + i + 1], axis=0),
                        in_=xr[:, :], in_offset=None), "sg", 16)
                    inc(GP.indirect_dma_start(
                        out=bufd, out_offset=bass.IndirectOffsetOnAxis(ap=didx[:, pl1 + i:pl1 + i + 1], axis=0),
                        in_=xr[:, :], in_offset=None), "sg", 16)
                    sc_mark[i] = cnt["sg"]
                SG_D1 = cnt["sg"]

                _ckpt(40)
                ipz_bf = E2(nc.sbuf_tensor("ipz_bf", [128, 8 * DI], BF16))
                outw_bf = E2(nc.sbuf_tensor("outw_bf", [128, 16 * DM], BF16))
                wst2 = E2(nc.sbuf_tensor("wst2", [128, DI], F32))
                wt2 = E2(nc.sbuf_tensor("wt2", [128, NH * 128], BF16))
                xt2 = E2(nc.sbuf_tensor("xt2", [128, NH * 128], BF16))
                s_bf = E2(nc.sbuf_tensor("s_bf", [64, NH * HD], BF16))
                clam = E2(nc.sbuf_tensor("clam", [64, 128], BF16))
                clamg = E2(nc.sbuf_tensor("clamg", [64, 128], BF16))
                z_ch = E2(nc.sbuf_tensor("z_ch", [128, DI], BF16))
                y_ch = E2(nc.sbuf_tensor("y_ch", [128, DI], F32))
                yg = E2(nc.sbuf_tensor("yg", [128, DI], BF16))
                ygfm = E2(nc.sbuf_tensor("ygfm", [128, NH * 128], BF16))
                u_ch = E2(nc.sbuf_tensor("u_ch", [128, DM], F32))
                rstd2 = E2(nc.sbuf_tensor("rstd2", [128, 2], F32))

                for d in range(8):
                    w(SY, "sa")
                    inc(SY.dma_start(out=wst2[:, :],
                                     in_=ipw_t.rearrange("(a p) e -> p a e", p=128)[:, d, 0:DI]), "sd", 16)
                    w(SC, "sd")
                    inc(SC.activation(out=ipz_bf[:, d * DI:(d + 1) * DI], in_=wst2[:, :], func=AF.Copy), "sa")
                for ic in range(16):
                    w(SY, "sv")
                    w(SY, "sa")
                    inc(SY.dma_start(out=wst2[:, 0:DM],
                                     in_=outw_t.rearrange("(a p) e -> p a e", p=128)[:, ic, :]), "sd", 16)
                    w(VE, "sd")
                    inc(VE.tensor_scalar(outw_bf[:, ic * DM:(ic + 1) * DM], wst2[:, 0:DM],
                                         rmsw[:, ic:ic + 1], None, op0=OP.mult), "sv")

                for c in range(NTCH):
                    cslice = slice(c * 128, (c + 1) * 128)
                    w(SY, "sp")
                    inc(SY.dma_start(out=wt2[:, :], in_=wt_sp[c]), "sd", 16)
                    inc(SY.dma_start(out=xt2[:, :], in_=xtm_sp[c]), "sd", 16)
                    w(SY, "sv")
                    w(SY, "sp")
                    inc(SY.dma_start(out=s_bf[:, :], in_=s_sp[c]), "sd", 16)
                    inc(SY.dma_start(out=u_ch[:, :],
                                     in_=u_sl[HALO + c * 128: HALO + (c + 1) * 128, :]), "sd", 16)

                    for eb in range(4):
                        pp = p_b if eb % 2 == 0 else p_c
                        w(TE, "sv")
                        for d in range(8):
                            instr = TE.matmul(pp[:, :],
                                              xn_fm[:, d * TH + HALO + c * 128: d * TH + HALO + (c + 1) * 128],
                                              ipz_bf[:, d * DI + eb * 512: d * DI + (eb + 1) * 512],
                                              start=(d == 0), stop=(d == 7))
                        inc(instr, "sp")
                        w(VE, "sp")
                        inc(VE.tensor_copy(z_ch[:, eb * 512:(eb + 1) * 512], pp[:, :]), "sv", sw=False)

                    for h in range(NH):
                        hs = slice(h * 128, (h + 1) * 128)
                        w(TE, "sv")
                        instr = TE.matmul(p_s[0:64, 0:128], e64b[:, h * 64:(h + 1) * 64],
                                          lam_bf[:, cslice], start=True, stop=True)
                        inc(instr, "sp")
                        instr = TE.matmul(p_s[0:64, 128:256], e64b[:, h * 64:(h + 1) * 64],
                                          lamg_bf[:, cslice], start=True, stop=True)
                        inc(instr, "sp")
                        w(VE, "sp")
                        inc(VE.tensor_tensor(clam[:, :], cfm_all[:, cslice], p_s[0:64, 0:128],
                                             op=OP.mult), "sv", sw=False)
                        inc(VE.tensor_tensor(clamg[:, :], cfm_all[:, cslice], p_s[0:64, 128:256],
                                             op=OP.mult), "sv", sw=False)
                        w(TE, "sv")
                        grp = h % 4
                        TE.matmul(p_y[:, grp * 128:(grp + 1) * 128], wt2[:, hs], xt2[:, hs],
                                  start=True, stop=False)
                        TE.matmul(p_y[:, grp * 128:(grp + 1) * 128], clam[:, :], s_bf[:, hs],
                                  start=False, stop=False)
                        instr = TE.matmul(p_y[:, grp * 128:(grp + 1) * 128], clamg[:, :], sin_bf[:, hs],
                                          start=False, stop=True)
                        inc(instr, "sp")
                        if grp == 3:
                            g = h // 4
                            w(VE, "sp")
                            inc(VE.tensor_copy(y_ch[:, g * 512:(g + 1) * 512], p_y[:, :]), "sv", sw=False)
                            w(TE, "sv")

                    if False:
                        pass
                    w(SC, "sv")
                    inc(SC.activation(out=yg[:, :], in_=y_ch[:, :], func=AF.Square), "sa")
                    inc(SC.activation(out=z_ch[:, :], in_=z_ch[:, :], func=AF.Sigmoid), "sa")
                    w(VE, "sa")
                    inc(VE.reduce_sum(rstd2[:, 0:1], yg[:, :], axis=AXX), "sv")
                    w(SC, "sv")
                    inc(SC.activation(out=rstd2[:, 1:2], in_=rstd2[:, 0:1], func=AF.Sqrt,
                                      scale=1.0 / DI, bias=epsb[:, 0:1]), "sa")
                    w(VE, "sa")
                    inc(VE.reciprocal(rstd2[:, 0:1], rstd2[:, 1:2]), "sv")
                    w(SC, "sv")
                    inc(SC.activation(out=y_ch[:, :], in_=y_ch[:, :], func=AF.Copy,
                                      scale=rstd2[:, 0:1]), "sa")
                    w(VE, "sa")
                    inc(VE.tensor_tensor(yg[:, :], y_ch[:, :], z_ch[:, :], op=OP.mult), "sv")

                    for h in range(NH):
                        sl4 = (h % 4) * 128
                        if h % 4 == 0:
                            w(TE, "sv")
                        instr = TE.transpose(p_a[:, sl4:sl4 + 128], yg[:, h * 128:(h + 1) * 128],
                                             identb[:, :])
                        if h % 4 == 3:
                            inc(instr, "sp")
                            w(VE, "sp")
                            inc(VE.tensor_copy(ygfm[:, (h - 3) * 128:(h + 1) * 128], p_a[:, :]), "sv")

                    for db in range(2):
                        pp = p_b if db == 0 else p_c
                        w(TE, "sv")
                        for ic in range(16):
                            instr = TE.matmul(pp[:, :], ygfm[:, ic * 128:(ic + 1) * 128],
                                              outw_bf[:, ic * DM + db * 512: ic * DM + (db + 1) * 512],
                                              start=(ic == 0), stop=(ic == 15))
                        inc(instr, "sp")
                        w(VE, "sp")
                        inc(VE.tensor_tensor(y_ch[:, db * 512:(db + 1) * 512], pp[:, :],
                                             u_ch[:, db * 512:(db + 1) * 512], op=OP.add), "sv")
                    w(SY, "sv")
                    inc(SY.dma_start(out=uattn[c * 128:(c + 1) * 128, :], in_=y_ch[:, 0:DM]), "sd", 16)

                barrier(TE, VE, SC, GP, SY)
                _close(st2)
                _close(stAB)
                _ckpt(45)

                _ckpt(50)
                st3 = _mkstack()
                E3 = st3.enter_context
                w_in_bf = E3(nc.sbuf_tensor("w_in_bf", [128, 8 * 2 * FF], BF16))
                w_out_bf = E3(nc.sbuf_tensor("w_out_bf", [128, 16 * DM], BF16))
                wst3 = E3(nc.sbuf_tensor("wst3", [128, FF], F32))
                ibuf0 = E3(nc.sbuf_tensor("ibuf0", [128, DM], BF16))
                ibuf1 = E3(nc.sbuf_tensor("ibuf1", [128, DM], BF16))
                inp_fm = E3(nc.sbuf_tensor("inp_fm", [128, 8 * 128], BF16))
                hid2 = E3(nc.sbuf_tensor("hid2", [128, FF], BF16))
                h2fm = E3(nc.sbuf_tensor("h2fm", [128, 16 * 128], BF16))
                gelt = E3(nc.sbuf_tensor("gelt", [128, 512], F32))
                gelt2 = E3(nc.sbuf_tensor("gelt2", [128, 512], F32))
                outeb = E3(nc.sbuf_tensor("outeb", [128, DM], BF16))

                for d in range(8):
                    for hh in range(2):
                        w(SY, "sa")
                        inc(SY.dma_start(out=wst3[:, :],
                                         in_=w_in_d.rearrange("(a p) e -> p a e", p=128)[:, d, hh * FF:(hh + 1) * FF]), "sd", 16)
                        w(SC, "sd")
                        inc(SC.activation(out=w_in_bf[:, d * 2 * FF + hh * FF: d * 2 * FF + (hh + 1) * FF],
                                          in_=wst3[:, :], func=AF.Copy), "sa")
                for f in range(16):
                    w(SY, "sa")
                    inc(SY.dma_start(out=wst3[:, 0:DM],
                                     in_=w_out_d.rearrange("(a p) e -> p a e", p=128)[:, f, :]), "sd", 16)
                    w(SC, "sd")
                    inc(SC.activation(out=w_out_bf[:, f * DM:(f + 1) * DM], in_=wst3[:, 0:DM],
                                      func=AF.Copy), "sa")

                w(SY, "sg", SG_D1)
                if KDBG2:
                    inc(SY.dma_start(out=ibuf0[:, :], in_=bufd[0:128, :]), "sd", 16)
                    dcap(1536, ibuf0[:, 0:512], 512)
                    dcap(2048, ibuf0[:, 512:1024], 512)
                for ct in range(NCT):
                    ib = ibuf0 if ct % 2 == 0 else ibuf1
                    if ct >= 2:
                        w(SY, "sp")
                    inc(SY.dma_start(out=ib[:, :], in_=bufd[ct * 128:(ct + 1) * 128, :]), "sd", 16)
                    w(TE, "sd")
                    w(TE, "sv")
                    for d in range(8):
                        sl4 = (d % 4) * 128
                        instr = TE.transpose(p_a[:, sl4:sl4 + 128], ib[:, d * 128:(d + 1) * 128],
                                             identb[:, :])
                        if d % 4 == 3:
                            inc(instr, "sp")
                            w(VE, "sp")
                            inc(VE.tensor_copy(inp_fm[:, (d - 3) * 128:(d + 1) * 128], p_a[:, :]), "sv", sw=False)
                            w(TE, "sv")
                    sa_mark = {}
                    sv_mark = {}
                    for jf in range(4):
                        pa = p_b if jf % 2 == 0 else p_y
                        pg = p_c if jf % 2 == 0 else p_s
                        ge = gelt if jf % 2 == 0 else gelt2
                        if jf < 2:
                            w(TE, "sv")
                            w(TE, "sa")
                        else:
                            w(TE, "sa", sa_mark[jf - 2])
                            w(TE, "sv", sv_mark[jf - 2])
                        for d in range(8):
                            instr = TE.matmul(pa[:, 0:512], inp_fm[:, d * 128:(d + 1) * 128],
                                              w_in_bf[:, d * 2 * FF + jf * 512: d * 2 * FF + (jf + 1) * 512],
                                              start=(d == 0), stop=(d == 7))
                        inc(instr, "sp")
                        for d in range(8):
                            instr = TE.matmul(pg[:, 0:512], inp_fm[:, d * 128:(d + 1) * 128],
                                              w_in_bf[:, d * 2 * FF + FF + jf * 512: d * 2 * FF + FF + (jf + 1) * 512],
                                              start=(d == 0), stop=(d == 7))
                        inc(instr, "sp")
                        w(SC, "sp", cnt["sp"] - 1)
                        inc(SC.activation(out=ge[:, :], in_=pa[:, 0:512], func=AF.Gelu), "sa")
                        sa_mark[jf] = cnt["sa"]
                        w(VE, "sa", sa_mark[jf])
                        w(VE, "sp")
                        inc(VE.tensor_tensor(hid2[:, jf * 512:(jf + 1) * 512], ge[:, :],
                                             pg[:, 0:512], op=OP.mult), "sv", sw=False)
                        sv_mark[jf] = cnt["sv"]
                    for f in range(16):
                        sl4 = (f % 4) * 128
                        if f % 4 == 0:
                            w(TE, "sv")
                        instr = TE.transpose(p_a[:, sl4:sl4 + 128], hid2[:, f * 128:(f + 1) * 128],
                                             identb[:, :])
                        if f % 4 == 3:
                            inc(instr, "sp")
                            w(VE, "sp")
                            inc(VE.tensor_copy(h2fm[:, (f - 3) * 128:(f + 1) * 128], p_a[:, :]), "sv", sw=False)
                    for db in range(2):
                        pp = p_b if db == 0 else p_c
                        w(TE, "sv")
                        for f in range(16):
                            instr = TE.matmul(pp[:, :], h2fm[:, f * 128:(f + 1) * 128],
                                              w_out_bf[:, f * DM + db * 512: f * DM + (db + 1) * 512],
                                              start=(f == 0), stop=(f == 15))
                        inc(instr, "sp")
                        w(SC, "sp")
                        inc(SC.activation(out=outeb[:, db * 512:(db + 1) * 512], in_=pp[:, :],
                                          func=AF.Copy), "sa")
                    w(SY, "sa")
                    inc(SY.dma_start(out=pool_sl[ct * 128:(ct + 1) * 128, :], in_=outeb[:, :]), "sd", 16)

                _ckpt(60)
                w(GP, "sd")
                inc(GP.collective_compute("AllGather", OP.bypass,
                                          replica_groups=[list(range(NCORES))],
                                          ins=[pool_sl.opt()], outs=[pool_all.opt()]), "sc")
                AG4_C = cnt["sc"]

                r0 = E3(nc.sbuf_tensor("r0", [128, DM], BF16))
                r1 = E3(nc.sbuf_tensor("r1", [128, DM], BF16))
                ffn = E3(nc.sbuf_tensor("ffn", [128, DM], F32))
                ua2 = E3(nc.sbuf_tensor("ua2", [128, DM], F32))
                if KDBG2:
                    w(SY, "sc")
                    inc(SY.dma_start(out=ibuf0[:, :], in_=pool_sl[0:128, :]), "sd", 16)
                    dcap(3712, ibuf0[:, 0:192], 192)
                    inc(SY.dma_start(out=ibuf0[:, :], in_=pool_all[0:128, :]), "sd", 16)
                    dcap(3904, ibuf0[:, 0:192], 192)
                inc(GP.memset(r0[:, :], 0.0), "sq")
                inc(GP.memset(r1[:, :], 0.0), "sq")
                w(GP, "sc", AG4_C)
                for k in range(NTCH):
                    if k > 0:
                        w(GP, "sv")
                    inc(GP.indirect_dma_start(out=r0[:, :], out_offset=None, in_=pool_all,
                                              in_offset=bass.IndirectOffsetOnAxis(ap=cgi[:, k:k + 1], axis=0)), "sg", 16)
                    w(GP, "sg")
                    inc(GP.indirect_dma_start(out=r1[:, :], out_offset=None, in_=pool_all,
                                              in_offset=bass.IndirectOffsetOnAxis(ap=cgi[:, NTCH + k:NTCH + k + 1], axis=0)), "sg", 16)
                    w(GP, "sg")
                    w(SY, "sv")
                    inc(SY.dma_start(out=ua2[:, :], in_=uattn[k * 128:(k + 1) * 128, :]), "sd", 16)
                    w(VE, "sg")
                    w(VE, "sd")
                    w(VE, "sq")
                    inc(VE.tensor_scalar(ffn[:, :], r0[:, :], ga[:, k * 2:k * 2 + 1], None,
                                         op0=OP.mult), "sv")
                    inc(VE.scalar_tensor_tensor(ffn[:, :], r1[:, :], ga[:, k * 2 + 1:k * 2 + 2],
                                                ffn[:, :], op0=OP.mult, op1=OP.add), "sv")
                    inc(VE.tensor_tensor(ffn[:, :], ffn[:, :], ua2[:, :], op=OP.add), "sv")
                    if k == 0:
                        dcap(2688, r0[:, 0:512], 512)
                        dcap(3200, ffn[:, 0:256], 256)
                    w(SY, "sv")
                    inc(SY.dma_start(out=yout[k * 128:(k + 1) * 128, :], in_=ffn[:, :]), "sd", 16)

                w(SY, "sd")
                w(SY, "sg")
                _close(st3)
            except _StopEmit:
                if _os.environ.get("KDBG") == "uattn" and KPH >= 41:
                    barrier(TE, VE, SC, GP, SY)
                    for k in range(NTCH):
                        inc(SY.dma_start(out=yout[k * 128:(k + 1) * 128, :],
                                         in_=uattn[k * 128:(k + 1) * 128, :]), "sd", 16)
                barrier(TE, VE, SC, GP, SY)
                w(SY, "sc")
                for _s in reversed(_open):
                    _s.close()
                _open.clear()
    return nc



_NC_CACHE = {}


def _get_nc():
    if "nc" not in _NC_CACHE:
        _NC_CACHE["nc"] = build_nc()
    return _NC_CACHE["nc"]


def _stage(inputs):
    bf = ml_dtypes.bfloat16
    f32 = np.float32
    u = np.ascontiguousarray(inputs["L0_attn_u"], dtype=f32).reshape(NCORES * TOK, DM)
    ipw = np.ascontiguousarray(np.asarray(inputs["L0_attn_in_proj_w"], dtype=f32).T)
    outw = np.ascontiguousarray(np.asarray(inputs["L0_attn_out_w"], dtype=f32).T)
    cw = np.asarray(inputs["L0_attn_conv_w"], dtype=f32)
    cb = np.asarray(inputs["L0_attn_conv_b"], dtype=f32)
    convw = np.ascontiguousarray(cw.reshape(NEC, 128, 4).transpose(1, 0, 2).reshape(128, NEC * 4))
    convb = np.ascontiguousarray(cb.reshape(NEC, 128).T)
    dtb = np.asarray(inputs["L0_attn_dt_bias"], dtype=f32).reshape(NH, 1)
    alog = np.asarray(inputs["L0_attn_A_log"], dtype=f32).reshape(NH, 1)
    dvec = np.asarray(inputs["L0_attn_D"], dtype=f32).reshape(1, NH)
    rmsw = np.ascontiguousarray(np.asarray(inputs["L0_attn_rms_w"], dtype=f32).reshape(NH, 128).T)
    wg = np.ascontiguousarray(np.asarray(inputs["L0_ffn_Wg"], dtype=f32))
    w_in = np.asarray(inputs["L0_ffn_W_in"], dtype=f32)
    w_out = np.asarray(inputs["L0_ffn_W_out"], dtype=f32)

    idx = np.arange(128)
    common = dict(
        ltmask=np.where(idx[:, None] <= idx[None, :], 0.0, -3.0e38).astype(f32),
        ltri=(idx[:, None] <= idx[None, :]).astype(f32),
        identb=np.eye(128).astype(bf),
        identf=np.eye(128).astype(f32),
        i16=np.eye(16, dtype=f32),
        ones1f=np.ones((1, 128), f32),
        onescol=np.ones((128, 1), f32),
        ones16_64=np.ones((16, 64), f32),
        iota8=np.broadcast_to(np.arange(8, dtype=f32), (128, 8)).copy(),
        iota8b=(np.broadcast_to(np.arange(8, dtype=f32), (128, 8)) + BIGI).astype(f32),
        convw=convw, convb=convb, dtb_col=dtb, alog_col=alog, dvec=dvec,
        rmsw_col=rmsw, wg=wg, ipw_t=ipw, outw_t=outw,
    )
    e16f = np.zeros((16, 16 * 128), f32)
    e16n = np.zeros((16, 16 * 128), f32)
    e64 = np.zeros((16, 16 * 64), f32)
    for h in range(16):
        e16f[h, h * 128:(h + 1) * 128] = 1.0
        e16n[h, h * 128:(h + 1) * 128] = -1.0
        e64[h, h * 64:(h + 1) * 64] = 1.0
    common["e16f"] = e16f
    common["e16n"] = e16n
    common["e64b"] = e64.astype(bf)

    in_maps = []
    for r in range(NCORES):
        m = dict(common)
        start = r * TOK
        if r % 2 == 0:
            usl = np.concatenate([np.zeros((HALO, DM), f32), u[start:start + TOK]], axis=0)
        else:
            usl = u[start - HALO:start + TOK]
        m["u_sl"] = np.ascontiguousarray(usl)
        m["w_in_e"] = np.ascontiguousarray(w_in[r])
        m["w_out_e"] = np.ascontiguousarray(w_out[r])
        pm = np.zeros((8, 2), f32)
        pm[:r, 0] = 1.0
        pm[:, 1] = 1.0
        m["prevmask"] = pm
        m["sgidx"] = (np.arange(64, dtype=np.uint32).reshape(64, 1)
                      + np.uint32(((r - 1) % NCORES) * 64))
        m["szero"] = np.full((64, 1), 1.0 if r % 2 == 1 else 0.0, f32)
        m["ecvec"] = np.full((128, 1), float(r * CAP), f32)
        for j in range(4):
            m[f"qtr{j}"] = (np.arange(128, dtype=f32) + CAP + j * 128).reshape(128, 1)
        m["qpool"] = (np.arange(128, dtype=f32) + (NCORES * CAP - 128)).reshape(128, 1)
        in_maps.append(m)
    return in_maps


def kernel(**inputs):
    nc = _get_nc()
    in_maps = _stage(inputs)
    res = run_bass_kernel_spmd(nc, in_maps, core_ids=list(range(NCORES)))
    globals()["_LAST_RES"] = res
    out = np.concatenate([res.results[r]["yout"] for r in range(NCORES)], axis=0)
    return out.reshape(4, 4096, DM)


# version 09 (final)
# speedup vs baseline: 1.0998x; 1.0998x over previous
import contextlib
import os as _os
import numpy as np
import ml_dtypes
import concourse.bass as bass
import concourse.mybir as mybir
from concourse.bass_utils import run_bass_kernel_spmd

F32 = mybir.dt.float32
BF16 = mybir.dt.bfloat16
U32 = mybir.dt.uint32
AF = mybir.ActivationFunctionType
OP = mybir.AluOpType
AXX = mybir.AxisListType.X
BT = bass.broadcast_tensor_aps

NCORES = 8
TOK = 2048
HALO = 3
DM = 1024
DIP = 4240
DI = 2048
NH = 16
HD = 128
NEC = 17
XW = DIP - DI
NTCH = 16
TILE = 512
NTILES = 4
CPT = 4
EXP = 8
FF = 2048
CAP = 5120
NCT = CAP // 128
OOB = 1.0e9
BIGI = 1.0e6
LN_EPS = 1e-5
RMS_EPS = 1e-5
TH = TOK + HALO
KPH = int(_os.environ.get("KPH", "99"))


class _StopEmit(Exception):
    pass


def _ckpt(n):
    if KPH < n:
        raise _StopEmit()


def build_nc():
    nc = bass.Bass("TRN2", target_bir_lowering=False)

    def din(name, shape, dt):
        return nc.dram_tensor(name, shape, dt, kind="ExternalInput").ap()

    u_sl = din("u_sl", [TH, DM], F32)
    ipw_t = din("ipw_t", [DM, DIP], F32)
    outw_t = din("outw_t", [DI, DM], F32)
    convw_d = din("convw", [128, NEC * 4], F32)
    convb_d = din("convb", [128, NEC], F32)
    dtb_d = din("dtb_col", [NH, 1], F32)
    alog_d = din("alog_col", [NH, 1], F32)
    dvec_d = din("dvec", [1, NH], F32)
    rmsw_d = din("rmsw_col", [128, NH], F32)
    wg_d = din("wg", [DM, EXP], F32)
    w_in_d = din("w_in_e", [DM, 2 * FF], F32)
    w_out_d = din("w_out_e", [FF, DM], F32)
    ltmask_d = din("ltmask", [128, 128], F32)
    ltri_d = din("ltri", [128, 128], F32)
    identb_d = din("identb", [128, 128], BF16)
    identf_d = din("identf", [128, 128], F32)
    e16f_d = din("e16f", [16, 16 * 128], F32)
    e16n_d = din("e16n", [16, 16 * 128], F32)
    e64b_d = din("e64b", [16, 16 * 64], BF16)
    i16_d = din("i16", [16, 16], F32)
    ones1f_d = din("ones1f", [1, 128], F32)
    onescol_d = din("onescol", [128, 1], F32)
    o1664_d = din("ones16_64", [16, 64], F32)
    prevmask_d = din("prevmask", [8, 2], F32)
    iota8_d = din("iota8", [128, 8], F32)
    iota8b_d = din("iota8b", [128, 8], F32)
    sgidx_d = din("sgidx", [64, 1], U32)
    szero_d = din("szero", [64, 1], F32)
    ecvec_d = din("ecvec", [128, 1], F32)
    qtr_d = [din(f"qtr{j}", [128, 1], F32) for j in range(4)]
    qpool_d = din("qpool", [128, 1], F32)

    yout = nc.dram_tensor("yout", [TOK, DM], F32, kind="ExternalOutput").ap()
    dbg = nc.dram_tensor("dbg", [128, 4096], F32, kind="ExternalOutput").ap()
    KDBG2 = _os.environ.get("KDBG2", "0") == "1"

    xn_slice = nc.dram_tensor("xn_slice", [TOK, DM], BF16).ap()
    cnt_sl = nc.dram_tensor("cnt_sl", [1, 16], F32).ap()
    d01_sl = nc.dram_tensor("d01_sl", [128, 2 * NTCH], F32).ap()
    sfin_sl = nc.dram_tensor("sfin_sl", [64, NH * HD], F32).ap()
    pool_sl = nc.dram_tensor("pool_sl", [CAP, DM], BF16).ap()
    cnt_all = nc.dram_tensor("cnt_all", [8, 16], F32).ap()
    guard0 = nc.dram_tensor("guard0", [4096, 1024], F32).ap()
    d01_all = nc.dram_tensor("d01_all", [NCORES * 128, 2 * NTCH], F32).ap()
    guard1 = nc.dram_tensor("guard1", [4096, 1024], F32).ap()
    sfin_all = nc.dram_tensor("sfin_all", [NCORES * 64, NH * HD], F32).ap()
    guard2 = nc.dram_tensor("guard2", [4096, 1024], F32).ap()
    xn_all = nc.dram_tensor("xn_all", [NCORES * TOK, DM], BF16).ap()
    guard3 = nc.dram_tensor("guard3", [4096, 1024], F32).ap()
    pool_all = nc.dram_tensor("pool_all", [NCORES * CAP, DM], BF16).ap()
    guard4 = nc.dram_tensor("guard4", [4096, 1024], F32).ap()
    bufd = nc.dram_tensor("bufd", [CAP + 512, DM], BF16).ap()
    wt_sp = nc.dram_tensor("wt_sp", [NTCH, 128, NH * 128], BF16).ap()
    xtm_sp = nc.dram_tensor("xtm_sp", [NTCH, 128, NH * 128], BF16).ap()
    s_sp = nc.dram_tensor("s_sp", [NTCH, 64, NH * HD], BF16).ap()
    uattn = nc.dram_tensor("uattn", [TOK, DM], F32).ap()

    es = contextlib.ExitStack()
    with es:
        E = es.enter_context
        sems = {}
        cnt = {}
        for s in ["sd", "sg", "sp", "sv", "sa", "sq", "sc"]:
            sems[s] = E(nc.semaphore(s))
            cnt[s] = 0

        _selfw = {"sv": "vector", "sa": "scalar", "sq": "gpsimd"}

        def inc(instr, s, n=1, sw=True):
            instr.then_inc(sems[s], n)
            cnt[s] += n
            if sw and s in _selfw:
                getattr(nc, _selfw[s]).wait_ge(sems[s], cnt[s])

        def w(eng, s, val=None):
            v = cnt[s] if val is None else val
            if v > 0:
                eng.wait_ge(sems[s], v)

        def barrier(*engs):
            for e in engs:
                for s in ["sd", "sg", "sp", "sv", "sa", "sq"]:
                    w(e, s)

        TE, VE, SC, GP, SY = nc.tensor, nc.vector, nc.scalar, nc.gpsimd, nc.sync

        def dcap(col, src_ap, width, parts=128):
            if not KDBG2:
                return
            w(VE, "sa")
            w(VE, "sp")
            w(VE, "sd")
            w(VE, "sg")
            w(VE, "sq")
            inc(VE.tensor_copy(dbg_st[0:parts, 0:width], src_ap), "sv")
            w(SY, "sv")
            inc(SY.dma_start(out=dbg[0:parts, col:col + width], in_=dbg_st[0:parts, 0:width]), "sd", 16)
            w(VE, "sd")

        d01_acc = E(nc.sbuf_tensor("d01_acc", [128, 2 * NTCH], F32))
        ltmask = E(nc.sbuf_tensor("c_ltmask", [128, 128], F32))
        ltri = E(nc.sbuf_tensor("c_ltri", [128, 128], F32))
        identb = E(nc.sbuf_tensor("c_identb", [128, 128], BF16))
        identf = E(nc.sbuf_tensor("c_identf", [128, 128], F32))
        e16f = E(nc.sbuf_tensor("c_e16f", [16, 16 * 128], F32))
        e16n = E(nc.sbuf_tensor("c_e16n", [16, 16 * 128], F32))
        e64b = E(nc.sbuf_tensor("c_e64b", [16, 16 * 64], BF16))
        i16 = E(nc.sbuf_tensor("c_i16", [16, 16], F32))
        ones1f = E(nc.sbuf_tensor("c_ones1f", [1, 128], F32))
        onescol = E(nc.sbuf_tensor("c_onescol", [128, 1], F32))
        o1664 = E(nc.sbuf_tensor("c_o1664", [16, 64], F32))
        prevmask = E(nc.sbuf_tensor("c_prevmask", [8, 2], F32))
        iota8 = E(nc.sbuf_tensor("c_iota8", [128, 8], F32))
        iota8b = E(nc.sbuf_tensor("c_iota8b", [128, 8], F32))
        sgidx = E(nc.sbuf_tensor("c_sgidx", [64, 1], U32))
        szero = E(nc.sbuf_tensor("c_szero", [64, 1], F32))
        ecvec = E(nc.sbuf_tensor("c_ecvec", [128, 1], F32))
        qtr = [E(nc.sbuf_tensor(f"c_qtr{j}", [128, 1], F32)) for j in range(4)]
        qpool = E(nc.sbuf_tensor("c_qpool", [128, 1], F32))
        dtb = E(nc.sbuf_tensor("c_dtb", [NH, 1], F32))
        alog = E(nc.sbuf_tensor("c_alog", [NH, 1], F32))
        dvec = E(nc.sbuf_tensor("c_dvec", [1, NH], F32))
        rmsw = E(nc.sbuf_tensor("c_rmsw", [128, NH], F32))
        convw = E(nc.sbuf_tensor("c_convw", [128, NEC * 4], F32))
        convb = E(nc.sbuf_tensor("c_convb", [128, NEC], F32))
        wgb = E(nc.sbuf_tensor("wgb", [128, 8 * 8], BF16))
        expa = E(nc.sbuf_tensor("expa", [NH, 1], F32))
        dbc = E(nc.sbuf_tensor("dbc", [128, NH], F32))
        wdi = E(nc.sbuf_tensor("wdi", [128, NH * 128], BF16))
        logits = E(nc.sbuf_tensor("logits", [128, NTCH * 8], F32))
        cump = E(nc.sbuf_tensor("cump", [16, NTCH * 128], F32))
        lam_bf = E(nc.sbuf_tensor("lam_bf", [16, NTCH * 128], BF16))
        lamg_bf = E(nc.sbuf_tensor("lamg_bf", [16, NTCH * 128], BF16))
        bendx = E(nc.sbuf_tensor("bendx", [16, NTCH + 1], F32))
        gprev = E(nc.sbuf_tensor("gprev", [16, NTCH], F32))
        cfm_all = E(nc.sbuf_tensor("cfm_all", [64, NTCH * 128], BF16))
        oh1a = E(nc.sbuf_tensor("oh1a", [128, NTCH * 8], F32))
        oh2a = E(nc.sbuf_tensor("oh2a", [128, NTCH * 8], F32))
        lp0a = E(nc.sbuf_tensor("lp0a", [128, NTCH * 8], F32))
        lp1a = E(nc.sbuf_tensor("lp1a", [128, NTCH * 8], F32))
        va = E(nc.sbuf_tensor("va", [128, NTCH * 2], F32))
        idxa = E(nc.sbuf_tensor("idxa", [128, NTCH * 2], F32))
        ga = E(nc.sbuf_tensor("ga", [128, NTCH * 2], F32))
        cgi = E(nc.sbuf_tensor("cgi", [128, NTCH * 2], U32))
        base_run = E(nc.sbuf_tensor("base_run", [1, 16], F32))
        gbrow = E(nc.sbuf_tensor("gbrow", [1, 16], F32))
        gb_bc = E(nc.sbuf_tensor("gb_bc", [128, 16], F32))
        sin_bf = E(nc.sbuf_tensor("sin_bf", [64, NH * HD], BF16))
        didx = E(nc.sbuf_tensor("didx", [128, 4 * 128], U32))
        zt = E(nc.sbuf_tensor("zt", [128, DM], BF16))
        rr6 = E(nc.sbuf_tensor("rr6", [128, 8], F32))
        epsb = E(nc.sbuf_tensor("epsb", [128, 1], F32))
        dbg_st = E(nc.sbuf_tensor("dbg_st", [128, 512], F32))
        rc = E(nc.sbuf_tensor("rc", [128, 12], F32))

        if _os.environ.get("KADDR") == "1":
            for _nm, _h in [("rc", rc), ("d01_acc", d01_acc), ("cgi", cgi), ("ga", ga),
                            ("dbg_st", dbg_st), ("epsb", epsb), ("zt", zt), ("didx", didx),
                            ("scratch", nc.dma_scratch), ("ltmask", ltmask)]:
                try:
                    _ml = nc.lookup_mloc(_h)
                    print("ADDR", _nm, _ml.addr, getattr(_ml, "size_bytes", None))
                except Exception as _e:
                    print("ADDR", _nm, "err", _e)

        p_a = E(nc.psum_tensor("p_a", [128, 4 * 128], BF16))
        p_b = E(nc.psum_tensor("p_b", [128, 512], F32))
        p_c = E(nc.psum_tensor("p_c", [128, 512], F32))
        p_m = E(nc.psum_tensor("p_m", [128, 128], F32))
        p_g = E(nc.psum_tensor("p_g", [128, 128], F32))
        p_y = E(nc.psum_tensor("p_y", [128, 512], F32))
        p_ds = E(nc.psum_tensor("p_ds", [64, 512], F32))
        p_s = E(nc.psum_tensor("p_s", [128, 512], F32))

        _open = []

        def _mkstack():
            s = contextlib.ExitStack()
            _open.append(s)
            return s

        def _close(s):
            _open.remove(s)
            s.close()

        stAB = _mkstack()
        xn_fm = stAB.enter_context(nc.sbuf_tensor("xn_fm", [128, 8 * TH], BF16))
        stW = _mkstack()
        ipw_bf = stW.enter_context(nc.sbuf_tensor("ipw_bf", [128, 8 * XW], BF16))

        with nc.Block() as block:
            try:
                stA = _mkstack()
                EA = stA.enter_context
                wstage = EA(nc.sbuf_tensor("wstage", [128, XW], F32))
                u_tm = EA(nc.sbuf_tensor("u_tm", [128, DM], F32))
                sq_tm = EA(nc.sbuf_tensor("sq_tm", [128, DM], BF16))
                xn_tm = EA(nc.sbuf_tensor("xn_tm", [128, DM], BF16))
                lnst = EA(nc.sbuf_tensor("lnst", [128, 4], F32))
                cntsb = EA(nc.sbuf_tensor("cntsb", [8, 16], F32))

                for dst, src in [
                    (ltmask, ltmask_d), (ltri, ltri_d), (identb, identb_d),
                    (identf, identf_d), (e16f, e16f_d), (e16n, e16n_d),
                    (e64b, e64b_d), (i16, i16_d), (ones1f, ones1f_d),
                    (onescol, onescol_d), (o1664, o1664_d), (prevmask, prevmask_d),
                    (iota8, iota8_d), (iota8b, iota8b_d), (sgidx, sgidx_d),
                    (szero, szero_d), (ecvec, ecvec_d), (qtr[0], qtr_d[0]),
                    (qtr[1], qtr_d[1]), (qtr[2], qtr_d[2]), (qtr[3], qtr_d[3]),
                    (qpool, qpool_d), (dtb, dtb_d),
                    (alog, alog_d), (dvec, dvec_d), (rmsw, rmsw_d),
                    (convw, convw_d), (convb, convb_d),
                ]:
                    inc(SY.dma_start(out=dst[:, :], in_=src), "sd", 16)

                w(SC, "sd")
                inc(SC.activation(out=expa[:, :], in_=alog[:, :], func=AF.Exp), "sa")
                inc(GP.memset(zt[:, :], 0.0), "sq")
                inc(GP.memset(epsb[:, :], LN_EPS), "sq")
                inc(GP.memset(base_run[:, :], 0.0), "sq")
                inc(GP.memset(bendx[:, 0:1], 0.0), "sq")
                inc(GP.memset(d01_acc[:, :], 7.0), "sq")
                w(GP, "sq")
                for i in range(NCT):
                    inc(GP.dma_start(out=bufd[i * 128:(i + 1) * 128, :], in_=zt[:, :]), "sg", 16)

                _ckpt(5)
                w(TE, "sd")
                inc(TE.matmul(p_m[:, 0:16], ones1f[:, :], dvec[:, :], start=True, stop=True), "sp")
                w(VE, "sp")
                inc(VE.tensor_copy(dbc[:, :], p_m[:, 0:16]), "sv")
                for h in range(NH):
                    inc(VE.tensor_scalar(wdi[:, h * 128:(h + 1) * 128], identb[:, :],
                                         dbc[:, h:h + 1], None, op0=OP.mult), "sv")

                _ckpt(7)
                for d in range(8):
                    w(SY, "sa")
                    inc(SY.dma_start(out=wstage[:, :],
                                     in_=ipw_t.rearrange("(a p) e -> p a e", p=128)[:, d, DI:DIP]), "sd", 16)
                    w(SC, "sd")
                    inc(SC.activation(out=ipw_bf[:, d * XW:(d + 1) * XW], in_=wstage[:, :], func=AF.Copy), "sa")
                w(SY, "sa")
                inc(SY.dma_start(out=wstage[:, 0:64],
                                 in_=wg_d.rearrange("(a p) e -> p a e", p=128)), "sd", 16)
                w(SC, "sd")
                inc(SC.activation(out=wgb[:, :], in_=wstage[:, 0:64], func=AF.Copy), "sa")

                _ckpt(9)
                for it in range(17):
                    rows = u_sl[0:128, :] if it == 0 else \
                        u_sl[HALO + (it - 1) * 128: HALO + it * 128, :]
                    w(SY, "sp")
                    w(SY, "sv")
                    inc(SY.dma_start(out=u_tm[:, :], in_=rows), "sd", 16)
                    w(VE, "sd")
                    inc(VE.reduce_sum(lnst[:, 0:1], u_tm[:, :], axis=AXX), "sv")
                    w(SC, "sd")
                    inc(SC.activation(out=sq_tm[:, :], in_=u_tm[:, :], func=AF.Square), "sa")
                    w(VE, "sa")
                    inc(VE.reduce_sum(lnst[:, 1:2], sq_tm[:, :], axis=AXX), "sv")
                    inc(VE.tensor_scalar(lnst[:, 2:3], lnst[:, 0:1], lnst[:, 0:1], 1.0 / (DM * DM),
                                         op0=OP.mult, op1=OP.mult), "sv")
                    inc(VE.scalar_tensor_tensor(lnst[:, 3:4], lnst[:, 1:2], 1.0 / DM,
                                                lnst[:, 2:3], op0=OP.mult, op1=OP.subtract), "sv")
                    w(SC, "sv")
                    inc(SC.activation(out=lnst[:, 2:3], in_=lnst[:, 3:4], func=AF.Sqrt, bias=epsb[:, 0:1]), "sa")
                    w(VE, "sa")
                    inc(VE.reciprocal(lnst[:, 3:4], lnst[:, 2:3]), "sv")
                    inc(VE.tensor_scalar(lnst[:, 0:1], lnst[:, 0:1], 1.0 / DM, None, op0=OP.mult), "sv")
                    inc(VE.tensor_scalar(xn_tm[:, :], u_tm[:, :], lnst[:, 0:1], lnst[:, 3:4],
                                         op0=OP.subtract, op1=OP.mult), "sv")
                    w(TE, "sv")
                    for half in range(2):
                        for d in range(4):
                            dd = half * 4 + d
                            instr = TE.transpose(p_a[:, d * 128:(d + 1) * 128],
                                                 xn_tm[:, dd * 128:(dd + 1) * 128], identb[:, :])
                        inc(instr, "sp")
                        w(VE, "sp")
                        for d in range(4):
                            dd = half * 4 + d
                            if it == 0:
                                dst = xn_fm[:, dd * TH + 0: dd * TH + 3]
                                srcp = p_a[:, d * 128 + 0: d * 128 + 3]
                            else:
                                k = it - 1
                                dst = xn_fm[:, dd * TH + HALO + k * 128: dd * TH + HALO + (k + 1) * 128]
                                srcp = p_a[:, d * 128:(d + 1) * 128]
                            inc(VE.tensor_copy(dst, srcp), "sv")
                        if half == 0:
                            w(TE, "sv")
                    if it > 0:
                        k = it - 1
                        w(SY, "sv")
                        inc(SY.dma_start(out=xn_slice[k * 128:(k + 1) * 128, :], in_=xn_tm[:, :]), "sd", 16)

                _ckpt(11)
                for k in range(NTCH):
                    w(TE, "sa")
                    w(TE, "sv")
                    for d in range(8):
                        instr = TE.matmul(p_m[:, 0:8],
                                          xn_fm[:, d * TH + HALO + k * 128: d * TH + HALO + (k + 1) * 128],
                                          wgb[:, d * 8:(d + 1) * 8],
                                          start=(d == 0), stop=(d == 7))
                    inc(instr, "sp")
                    w(VE, "sp")
                    inc(VE.tensor_copy(logits[:, k * 8:(k + 1) * 8], p_m[:, 0:8]), "sv")

                _ckpt(12)
                dcap(0, xn_fm[:, 0 * TH + 3: 0 * TH + 3 + 512], 512)
                for k in range(NTCH):
                    lg = logits[:, k * 8:(k + 1) * 8]
                    w(VE, "sv")
                    inc(VE.reduce_max(rc[:, 0:1], lg, axis=AXX, negate=True), "sv")
                    w(SC, "sv")
                    inc(SC.activation(out=rr6[:, :], in_=lg, func=AF.Exp, bias=rc[:, 0:1]), "sa")
                    w(VE, "sa")
                    inc(VE.reduce_sum(rc[:, 1:2], rr6[:, :], axis=AXX), "sv")
                    inc(VE.reciprocal(rc[:, 2:3], rc[:, 1:2]), "sv")
                    inc(VE.tensor_scalar(rr6[:, :], rr6[:, :], rc[:, 2:3], None, op0=OP.mult), "sv")
                    inc(VE.reduce_max(va[:, k * 2:k * 2 + 1], rr6[:, :], axis=AXX), "sv")
                    inc(VE.tensor_scalar(oh1a[:, k * 8:(k + 1) * 8], rr6[:, :],
                                         va[:, k * 2:k * 2 + 1], None, op0=OP.is_equal), "sv")
                    inc(VE.scalar_tensor_tensor(rc[:, 4:12], oh1a[:, k * 8:(k + 1) * 8], -BIGI,
                                                iota8b[:, :], op0=OP.mult, op1=OP.add), "sv")
                    inc(VE.tensor_reduce(idxa[:, k * 2:k * 2 + 1], rc[:, 4:12], axis=AXX, op=OP.min), "sv")
                    inc(VE.tensor_scalar(oh1a[:, k * 8:(k + 1) * 8], iota8[:, :],
                                         idxa[:, k * 2:k * 2 + 1], None, op0=OP.is_equal), "sv")
                    inc(VE.scalar_tensor_tensor(rr6[:, :], oh1a[:, k * 8:(k + 1) * 8], -BIGI,
                                                rr6[:, :], op0=OP.mult, op1=OP.add), "sv")
                    inc(VE.reduce_max(va[:, k * 2 + 1:k * 2 + 2], rr6[:, :], axis=AXX), "sv")
                    inc(VE.tensor_scalar(oh2a[:, k * 8:(k + 1) * 8], rr6[:, :],
                                         va[:, k * 2 + 1:k * 2 + 2], None, op0=OP.is_equal), "sv")
                    inc(VE.scalar_tensor_tensor(rc[:, 4:12], oh2a[:, k * 8:(k + 1) * 8], -BIGI,
                                                iota8b[:, :], op0=OP.mult, op1=OP.add), "sv")
                    inc(VE.tensor_reduce(idxa[:, k * 2 + 1:k * 2 + 2], rc[:, 4:12], axis=AXX, op=OP.min), "sv")
                    inc(VE.tensor_scalar(oh2a[:, k * 8:(k + 1) * 8], iota8[:, :],
                                         idxa[:, k * 2 + 1:k * 2 + 2], None, op0=OP.is_equal), "sv")
                    for kk, oh, lp in ((0, oh1a, lp0a), (1, oh2a, lp1a)):
                        w(TE, "sv")
                        TE.matmul(p_m[:, 0:8], ltri[:, :], oh[:, k * 8:(k + 1) * 8],
                                  start=True, stop=False)
                        instr = TE.matmul(p_m[:, 0:8], ones1f[:, :],
                                          base_run[:, kk * 8:(kk + 1) * 8], start=False, stop=True)
                        inc(instr, "sp")
                        instr = TE.matmul(p_s[0:1, 0:8], onescol[:, :], oh[:, k * 8:(k + 1) * 8],
                                          start=True, stop=True)
                        inc(instr, "sp")
                        w(VE, "sp")
                        inc(VE.tensor_copy(lp[:, k * 8:(k + 1) * 8], p_m[:, 0:8]), "sv")
                        inc(VE.tensor_tensor(base_run[:, kk * 8:(kk + 1) * 8],
                                             base_run[:, kk * 8:(kk + 1) * 8],
                                             p_s[0:1, 0:8], op=OP.add), "sv")
                        w(TE, "sv")

                _ckpt(13)
                w(SY, "sv")
                inc(SY.dma_start(out=cnt_sl[:, :], in_=base_run[:, :]), "sd", 16)
                w(GP, "sd")
                inc(GP.collective_compute("AllGather", OP.bypass,
                                          replica_groups=[list(range(NCORES))],
                                          ins=[cnt_sl.opt()], outs=[cnt_all.opt()]), "sc")
                AG2_C = cnt["sc"]
                inc(GP.collective_compute("AllGather", OP.bypass,
                                          replica_groups=[list(range(NCORES))],
                                          ins=[xn_slice.opt()], outs=[xn_all.opt()]), "sc")
                AG1_C = cnt["sc"]

                _ckpt(14)
                w(SY, "sc", AG2_C)
                inc(SY.dma_start(out=cntsb[:, :], in_=cnt_all), "sd", 16)
                w(TE, "sd")
                instr = TE.matmul(p_s[0:1, 0:16], prevmask[:, 0:1], cntsb[:, :], start=True, stop=True)
                inc(instr, "sp")
                instr = TE.matmul(p_s[0:1, 16:32], prevmask[:, 1:2], cntsb[:, :], start=True, stop=True)
                inc(instr, "sp")
                w(VE, "sp")
                inc(VE.tensor_copy(gbrow[:, 0:8], p_s[0:1, 0:8]), "sv")
                inc(VE.tensor_copy(gbrow[:, 8:16], p_s[0:1, 16:24]), "sv")
                inc(VE.tensor_tensor(gbrow[:, 8:16], gbrow[:, 8:16], p_s[0:1, 8:16], op=OP.add), "sv")
                w(TE, "sv")
                instr = TE.matmul(p_s[:, 0:16], ones1f[:, :], gbrow[:, :], start=True, stop=True)
                inc(instr, "sp")
                w(VE, "sp")
                inc(VE.tensor_copy(gb_bc[:, :], p_s[:, 0:16]), "sv")

                for k in range(NTCH):
                    sl = slice(k * 8, (k + 1) * 8)
                    for kk, oh, lp in ((0, oh1a, lp0a), (1, oh2a, lp1a)):
                        inc(VE.tensor_tensor(rr6[:, :], lp[:, sl],
                                             gb_bc[:, kk * 8:(kk + 1) * 8], op=OP.add), "sv")
                        inc(VE.tensor_tensor(rr6[:, :], rr6[:, :], oh[:, sl], op=OP.mult), "sv")
                        inc(VE.reduce_sum(rc[:, 0:1], rr6[:, :], axis=AXX), "sv")
                        inc(VE.tensor_scalar(rc[:, 1:2], rc[:, 0:1], float(CAP), None,
                                             op0=OP.is_le), "sv")
                        if kk == 0:
                            inc(VE.tensor_copy(rc[:, 10:11], rc[:, 1:2]), "sv")
                        inc(VE.tensor_scalar(rc[:, 2:3], rc[:, 0:1], -1.0, None, op0=OP.add), "sv")
                        inc(VE.scalar_tensor_tensor(rc[:, 3:4], idxa[:, k * 2 + kk:k * 2 + kk + 1],
                                                    float(CAP), rc[:, 2:3], op0=OP.mult, op1=OP.add), "sv")
                        inc(VE.tensor_scalar(rc[:, 4:5], rc[:, 1:2], -OOB, OOB,
                                             op0=OP.mult, op1=OP.add), "sv")
                        inc(VE.scalar_tensor_tensor(rc[:, 5 + kk:6 + kk], rc[:, 3:4], rc[:, 1:2],
                                                    rc[:, 4:5], op0=OP.mult, op1=OP.add), "sv")
                        inc(VE.tensor_scalar(ga[:, k * 2 + kk:k * 2 + kk + 1],
                                             va[:, k * 2 + kk:k * 2 + kk + 1],
                                             rc[:, 1:2], None, op0=OP.mult), "sv")
                    inc(VE.tensor_tensor(rc[:, 7:8], ga[:, k * 2:k * 2 + 1],
                                         ga[:, k * 2 + 1:k * 2 + 2], op=OP.add), "sv")
                    inc(VE.tensor_scalar(rc[:, 7:8], rc[:, 7:8], 1e-9, None, op0=OP.max), "sv")
                    inc(VE.reciprocal(rc[:, 8:9], rc[:, 7:8]), "sv")
                    inc(VE.tensor_scalar(ga[:, k * 2:k * 2 + 1], ga[:, k * 2:k * 2 + 1],
                                         rc[:, 8:9], None, op0=OP.mult), "sv")
                    inc(VE.tensor_scalar(ga[:, k * 2 + 1:k * 2 + 2], ga[:, k * 2 + 1:k * 2 + 2],
                                         rc[:, 8:9], None, op0=OP.mult), "sv")
                    for kk2 in range(2):
                        inc(VE.tensor_scalar(rc[:, 9:10], rc[:, 1:2] if kk2 == 1 else rc[:, 10:11],
                                             -1.0, 1.0, op0=OP.mult, op1=OP.add), "sv")
                        inc(VE.tensor_scalar(rc[:, 9:10], rc[:, 9:10], qpool[:, 0:1], None,
                                             op0=OP.mult), "sv")
                        inc(VE.scalar_tensor_tensor(rc[:, 11:12], rc[:, 5 + kk2:6 + kk2],
                                                    rc[:, 1:2] if kk2 == 1 else rc[:, 10:11],
                                                    rc[:, 9:10], op0=OP.mult, op1=OP.add), "sv")
                        inc(VE.tensor_copy(cgi[:, kk2 * NTCH + k:kk2 * NTCH + k + 1], rc[:, 11:12]), "sv")
                    inc(VE.tensor_copy(d01_acc[:, 2 * k:2 * k + 1], rc[:, 5:6]), "sv")
                    inc(VE.tensor_copy(d01_acc[:, 2 * k + 1:2 * k + 2], rc[:, 6:7]), "sv")

                dcap(0, ga[:, :], 32)
                dcap(32, idxa[:, :], 32)
                dcap(64, va[:, :], 32)
                dcap(96, gb_bc[:, :], 16)
                dcap(112, lp0a[:, 0:8], 8)
                dcap(120, oh1a[:, 0:8], 8)
                dcap(128, cgi[:, :], 32)
                dcap(160, logits[:, 0:8], 8)
                dcap(3616, d01_acc[:, :], 32)
                _ckpt(15)
                w(SY, "sv")
                inc(SY.dma_start(out=d01_sl, in_=d01_acc[:, :]), "sd", 16)
                w(GP, "sd")
                inc(GP.collective_compute("AllGather", OP.bypass,
                                          replica_groups=[list(range(NCORES))],
                                          ins=[d01_sl.opt()], outs=[d01_all.opt()]), "sc")
                AG3_C = cnt["sc"]

                barrier(TE, VE, SC, GP, SY)
                _close(stA)

                _ckpt(20)
                st1 = _mkstack()
                E1 = st1.enter_context
                xbc_in = E1(nc.sbuf_tensor("xbc_in", [128, NEC * (TILE + HALO)], BF16))
                xbc_out = E1(nc.sbuf_tensor("xbc_out", [128, NEC * TILE], BF16))
                conv_sa = E1(nc.sbuf_tensor("conv_sa", [128, TILE], F32))
                conv_sb = E1(nc.sbuf_tensor("conv_sb", [128, TILE], F32))
                dtp = E1(nc.sbuf_tensor("dtp", [16, TILE], F32))
                wrtp = E1(nc.sbuf_tensor("wrtp", [128, 128], F32))
                wtmc = E1(nc.sbuf_tensor("wtmc", [128, 16], F32))
                btm = E1(nc.sbuf_tensor("btm", [128, 128], BF16))
                bw_tm = E1(nc.sbuf_tensor("bw_tm", [128, 64], BF16))
                msb = E1(nc.sbuf_tensor("msb", [128, 128], F32))
                mex = E1(nc.sbuf_tensor("mex", [128, 128], F32))
                mlhs = E1(nc.sbuf_tensor("mlhs", [16, 128], F32))
                wt_t = E1(nc.sbuf_tensor("wt_t", [128, NH * 128], BF16))
                xtm_t = E1(nc.sbuf_tensor("xtm_t", [128, NH * 128], BF16))
                sA = E1(nc.sbuf_tensor("sA", [64, NH * HD], F32))
                sB = E1(nc.sbuf_tensor("sB", [64, NH * HD], F32))
                s_tmp = E1(nc.sbuf_tensor("s_tmp", [64, 512], F32))
                s_spb = E1(nc.sbuf_tensor("s_spb", [64, NH * HD], BF16))
                g64 = E1(nc.sbuf_tensor("g64", [64, 16], F32))
                gcol = E1(nc.sbuf_tensor("gcol", [16, 2], F32))
                dg16 = E1(nc.sbuf_tensor("dg16", [16, 16], F32))
                s_cur, s_nxt = sA, sB

                inc(GP.memset(sA[:, :], 0.0), "sq")
                inc(GP.memset(wrtp[:, :], 0.0), "sq")
                barrier(TE, VE, SC, GP, SY)

                for kt in range(NTILES):
                    t0 = kt * TILE
                    if kt == 0:
                        for e in range(NEC):
                            pp = p_b if e % 2 == 0 else p_c
                            w(TE, "sv")
                            for d in range(8):
                                instr = TE.matmul(pp[:, 0:HALO],
                                                  ipw_bf[:, d * XW + e * 128: d * XW + (e + 1) * 128],
                                                  xn_fm[:, d * TH + 0: d * TH + HALO],
                                                  start=(d == 0), stop=(d == 7))
                            inc(instr, "sp")
                            w(VE, "sp")
                            inc(VE.tensor_copy(
                                xbc_in[:, e * (TILE + HALO): e * (TILE + HALO) + HALO], pp[:, 0:HALO]), "sv")
                    else:
                        xv = xbc_in[:, :].rearrange("p (e t) -> p e t", t=TILE + HALO)
                        inc(VE.tensor_copy(xv[:, :, 0:HALO], xv[:, :, TILE:TILE + HALO]), "sv")

                    for e in range(NEC):
                        pp = p_b if e % 2 == 0 else p_c
                        w(TE, "sv")
                        for d in range(8):
                            instr = TE.matmul(pp[:, :],
                                              ipw_bf[:, d * XW + e * 128: d * XW + (e + 1) * 128],
                                              xn_fm[:, d * TH + HALO + t0: d * TH + HALO + t0 + TILE],
                                              start=(d == 0), stop=(d == 7))
                        inc(instr, "sp")
                        w(VE, "sp")
                        inc(VE.tensor_copy(
                            xbc_in[:, e * (TILE + HALO) + HALO: (e + 1) * (TILE + HALO)], pp[:, :]), "sv", sw=False)

                    w(TE, "sv")
                    for d in range(8):
                        instr = TE.matmul(p_s[0:16, 0:TILE],
                                          ipw_bf[:, d * XW + NEC * 128: d * XW + NEC * 128 + 16],
                                          xn_fm[:, d * TH + HALO + t0: d * TH + HALO + t0 + TILE],
                                          start=(d == 0), stop=(d == 7))
                    inc(instr, "sp")
                    w(SC, "sp")
                    inc(SC.activation(out=dtp[:, :], in_=p_s[0:16, 0:TILE], func=AF.Abs,
                                      bias=dtb[:, 0:1]), "sa")
                    inc(SC.activation(out=dtp[:, :], in_=dtp[:, :], func=AF.Exp, scale=-1.0), "sa")
                    inc(SC.activation(out=dtp[:, :], in_=dtp[:, :], func=AF.Ln, bias=1.0), "sa")
                    inc(SC.activation(out=conv_sa[0:16, :], in_=p_s[0:16, 0:TILE], func=AF.Relu,
                                      bias=dtb[:, 0:1]), "sa")
                    w(VE, "sa")
                    inc(VE.tensor_tensor(dtp[:, :], dtp[:, :], conv_sa[0:16, :], op=OP.add), "sv")
                    inc(VE.tensor_scalar(dtp[:, :], dtp[:, :], expa[:, 0:1], None, op0=OP.mult), "sv")
                    for j in range(CPT):
                        c = kt * CPT + j
                        inc(VE.tensor_tensor_scan(cump[:, c * 128:(c + 1) * 128],
                                                  dtp[:, j * 128:(j + 1) * 128],
                                                  dtp[:, j * 128:(j + 1) * 128],
                                                  0.0, op0=OP.add, op1=OP.bypass), "sv")
                        w(SC, "sv")
                        inc(SC.activation(out=lam_bf[:, c * 128:(c + 1) * 128],
                                          in_=cump[:, c * 128:(c + 1) * 128], func=AF.Exp,
                                          scale=-1.0), "sa")
                        w(VE, "sa")
                        inc(VE.tensor_copy(bendx[:, c + 1:c + 2],
                                           cump[:, c * 128 + 127:c * 128 + 128]), "sv")

                    for e in range(NEC):
                        base = e * (TILE + HALO)
                        w(VE, "sv")
                        w(VE, "sa")
                        inc(VE.tensor_scalar(conv_sa[:, :], xbc_in[:, base + 0: base + TILE],
                                             convw[:, e * 4 + 0:e * 4 + 1], None, op0=OP.mult), "sv", sw=False)
                        inc(VE.scalar_tensor_tensor(conv_sb[:, :], xbc_in[:, base + 1: base + 1 + TILE],
                                                    convw[:, e * 4 + 1:e * 4 + 2], conv_sa[:, :],
                                                    op0=OP.mult, op1=OP.add), "sv", sw=False)
                        inc(VE.scalar_tensor_tensor(conv_sa[:, :], xbc_in[:, base + 2: base + 2 + TILE],
                                                    convw[:, e * 4 + 2:e * 4 + 3], conv_sb[:, :],
                                                    op0=OP.mult, op1=OP.add), "sv", sw=False)
                        inc(VE.scalar_tensor_tensor(conv_sb[:, :], xbc_in[:, base + 3: base + 3 + TILE],
                                                    convw[:, e * 4 + 3:e * 4 + 4], conv_sa[:, :],
                                                    op0=OP.mult, op1=OP.add), "sv", sw=False)
                        w(SC, "sv")
                        inc(SC.activation(out=xbc_out[:, e * TILE:(e + 1) * TILE], in_=conv_sb[:, :],
                                          func=AF.Silu, bias=convb[:, e:e + 1]), "sa")

                    if False:
                        pass
                    for j in range(CPT):
                        c = kt * CPT + j
                        cslice = slice(c * 128, (c + 1) * 128)
                        bc_sl = slice(16 * TILE + j * 128, 16 * TILE + (j + 1) * 128)
                        w(SC, "sv")
                        inc(SC.activation(out=gcol[:, 0:1], in_=cump[:, c * 128 + 127:c * 128 + 128],
                                          func=AF.Exp, scale=-1.0), "sa")
                        inc(SC.activation(out=gcol[:, 1:2], in_=cump[:, c * 128 + 127:c * 128 + 128],
                                          func=AF.Copy, scale=-1.0), "sa")
                        inc(SC.activation(out=wrtp[0:16, :], in_=cump[:, cslice],
                                          func=AF.Exp, bias=gcol[:, 1:2]), "sa")
                        w(VE, "sa")
                        inc(VE.tensor_scalar(dg16[:, :], i16[:, :], gcol[:, 0:1], None, op0=OP.mult), "sv")
                        w(TE, "sv")
                        w(TE, "sa")
                        instr = TE.matmul(p_s[0:64, 128:144], o1664[:, :], dg16[:, :],
                                          start=True, stop=True)
                        inc(instr, "sp")
                        instr = TE.transpose(p_s[:, 144:272], wrtp[:, :], identf[:, :])
                        inc(instr, "sp")
                        instr = TE.transpose(p_a[:, 0:128], xbc_out[:, bc_sl], identb[:, :])
                        inc(instr, "sp")
                        w(VE, "sp")
                        inc(VE.tensor_copy(g64[:, :], p_s[0:64, 128:144]), "sv")
                        inc(VE.tensor_copy(wtmc[:, :], p_s[:, 144:272][:, 0:16]), "sv")
                        inc(VE.tensor_copy(btm[:, :], p_a[:, 0:128]), "sv")
                        inc(VE.tensor_copy(cfm_all[:, cslice], xbc_out[64:128, bc_sl]), "sv")

                        for h in range(NH):
                            hs = slice(h * 128, (h + 1) * 128)
                            w(TE, "sv")
                            instr = TE.transpose(p_a[:, 128:256],
                                                 xbc_out[:, h * TILE + j * 128: h * TILE + (j + 1) * 128],
                                                 identb[:, :])
                            inc(instr, "sp")
                            w(VE, "sp")
                            inc(VE.tensor_copy(xtm_t[:, hs], p_a[:, 128:256]), "sv", sw=False)
                            inc(VE.tensor_scalar(mlhs[:, :], cump[:, cslice],
                                                 e16f[:, h * 128:h * 128 + 1], None, op0=OP.mult), "sv")
                            w(TE, "sv")
                            TE.matmul(p_m[:, :], mlhs[:, :], e16f[:, hs], start=True, stop=False)
                            instr = TE.matmul(p_m[:, :], e16n[:, hs], cump[:, cslice],
                                              start=False, stop=True)
                            inc(instr, "sp")
                            TE.matmul(p_g[:, :], xbc_out[0:64, bc_sl], cfm_all[:, cslice],
                                      start=True, stop=False)
                            instr = TE.matmul(p_g[:, :], wdi[:, hs], identb[:, :],
                                              start=False, stop=True)
                            inc(instr, "sp")
                            w(VE, "sp")
                            inc(VE.tensor_tensor(msb[:, :], p_m[:, :], ltmask[:, :], op=OP.add), "sv", sw=False)
                            w(SC, "sv")
                            inc(SC.activation(out=mex[:, :], in_=msb[:, :], func=AF.Exp), "sa")
                            w(VE, "sa")
                            inc(VE.tensor_tensor(wt_t[:, hs], mex[:, :], p_g[:, :], op=OP.mult), "sv", sw=False)
                            inc(VE.tensor_scalar(bw_tm[:, :], btm[:, 0:64], wtmc[:, h:h + 1],
                                                 None, op0=OP.mult), "sv", sw=False)
                            w(TE, "sv")
                            grp = h % 4
                            instr = TE.matmul(p_ds[:, grp * 128:(grp + 1) * 128], bw_tm[:, :],
                                              xtm_t[:, hs], start=True, stop=True)
                            inc(instr, "sp")
                            if grp == 3:
                                g = h // 4
                                gsl = slice(g * 512, (g + 1) * 512)
                                if g == 0:
                                    w(VE, "sd")
                                    inc(VE.tensor_copy(s_spb[:, :], s_cur[:, :]), "sv")
                                    w(SY, "sv")
                                    inc(SY.dma_start(out=s_sp[c], in_=s_spb[:, :]), "sd", 16)
                                sv_view = s_cur[:, gsl].rearrange("p (a b) -> p a b", b=128)
                                gv_view = g64[:, g * 4:(g + 1) * 4].rearrange("p (a o) -> p a o", o=1)
                                aa, bb = BT(sv_view, gv_view)
                                w(VE, "sd")
                                inc(VE.tensor_tensor(s_tmp[:, :].rearrange("p (a b) -> p a b", b=128),
                                                     aa, bb, op=OP.mult), "sv")
                                w(VE, "sp")
                                inc(VE.tensor_tensor(s_nxt[:, gsl], s_tmp[:, :], p_ds[:, :], op=OP.add), "sv")
                                w(TE, "sv")

                        if False:
                            pass
                        w(SY, "sv")
                        inc(SY.dma_start(out=wt_sp[c], in_=wt_t[:, :]), "sd", 16)
                        inc(SY.dma_start(out=xtm_sp[c], in_=xtm_t[:, :]), "sd", 16)
                        s_cur, s_nxt = s_nxt, s_cur
                        w(VE, "sd")
                        w(TE, "sd")

                w(SY, "sv")
                inc(SY.dma_start(out=sfin_sl, in_=s_cur[:, :]), "sd", 16)
                w(GP, "sd")
                inc(GP.collective_compute("AllGather", OP.bypass,
                                          replica_groups=[list(range(NCORES))],
                                          ins=[sfin_sl.opt()], outs=[sfin_all.opt()]), "sc")
                AGS_C = cnt["sc"]
                w(GP, "sc", AGS_C)
                w(GP, "sv")
                inc(GP.indirect_dma_start(out=s_nxt[:, :], out_offset=None, in_=sfin_all,
                                          in_offset=bass.IndirectOffsetOnAxis(ap=sgidx[:, 0:1], axis=0)), "sg", 16)
                w(VE, "sg")
                inc(VE.tensor_scalar(sin_bf[:, :], s_nxt[:, :], szero[:, 0:1], None, op0=OP.mult), "sv")

                inc(VE.tensor_tensor_scan(gprev[:, :], bendx[:, 0:NTCH], bendx[:, 0:NTCH],
                                          0.0, op0=OP.add, op1=OP.bypass), "sv")
                w(SC, "sv")
                inc(SC.activation(out=gprev[:, :], in_=gprev[:, :], func=AF.Exp, scale=-1.0), "sa")
                w(VE, "sa")
                for c in range(NTCH):
                    inc(VE.tensor_scalar(lamg_bf[:, c * 128:(c + 1) * 128],
                                         lam_bf[:, c * 128:(c + 1) * 128],
                                         gprev[:, c:c + 1], None, op0=OP.mult), "sv")

                barrier(TE, VE, SC, GP, SY)
                _close(st1)
                _close(stW)

                _ckpt(30)
                st2 = _mkstack()
                E2 = st2.enter_context
                xrow0 = E2(nc.sbuf_tensor("xrow0", [128, DM], BF16))
                xrow1 = E2(nc.sbuf_tensor("xrow1", [128, DM], BF16))
                dall = E2(nc.sbuf_tensor("dall", [128, 128 * 2], F32))
                dsc = E2(nc.sbuf_tensor("dsc", [128, 128 * 4], F32))

                w(SY, "sc", AG3_C)
                inc(SY.dma_start(out=dall[:, :].rearrange("p (r m) -> p r m", m=2 * NTCH),
                                 in_=d01_all.rearrange("(r p) m -> p r m", p=128)), "sd", 16)
                for kk in range(2):
                    dk = dall[:, :].rearrange("p (n c) -> p n c", c=2)[:, :, kk]
                    t_ = dsc[:, 0:128]
                    v1_ = dsc[:, 128:256]
                    f_ = dsc[:, 256:384]
                    g_ = dsc[:, 384:512]
                    w(VE, "sd")
                    inc(VE.tensor_scalar(t_, dk, ecvec[:, 0:1], None, op0=OP.subtract), "sv")
                    inc(VE.tensor_scalar(v1_, t_, 0.0, None, op0=OP.is_ge), "sv")
                    inc(VE.tensor_scalar(f_, t_, float(CAP), None, op0=OP.is_lt), "sv")
                    inc(VE.tensor_tensor(v1_, v1_, f_, op=OP.mult), "sv")
                    inc(VE.tensor_tensor(t_, t_, v1_, op=OP.mult), "sv")
                    inc(VE.tensor_scalar(f_, v1_, -1.0, 1.0, op0=OP.mult, op1=OP.add), "sv")
                    for par in range(2):
                        qc = qtr[kk * 2 + par]
                        inc(VE.tensor_scalar(g_, f_, qc[:, 0:1], None, op0=OP.mult), "sv")
                        inc(VE.tensor_tensor(didx[:, (kk * 2 + par) * 128:(kk * 2 + par + 1) * 128],
                                             t_, g_, op=OP.add), "sv")

                dcap(3520, d01_acc[:, 0:32], 32)
                w(SY, "sv")
                w(SY, "sc")
                inc(SY.dma_start(out=dsc[:, 0:32], in_=d01_sl), "sd", 16)
                inc(SY.dma_start(out=dsc[:, 32:64], in_=d01_all[0:128, :]), "sd", 16)
                dcap(3552, dsc[:, 0:64], 64)
                dcap(3456, didx[:, 0:16], 16)
                dcap(3472, didx[:, 256:272], 16)
                dcap(3488, dall[:, 0:32], 32)
                w(GP, "sv")
                w(GP, "sc", AG1_C)
                w(GP, "sg")
                SG_D0 = cnt["sg"]
                for i in range(128):
                    xr = xrow0 if i % 2 == 0 else xrow1
                    if i >= 2:
                        w(GP, "sg", SG_D0 + 16 * (3 * (i - 2) + 3))
                    inc(GP.dma_start(out=xr[:, :], in_=xn_all[i * 128:(i + 1) * 128, :]), "sg", 16)
                    w(GP, "sg")
                    pl0 = (0 * 2 + i % 2) * 128
                    pl1 = (1 * 2 + i % 2) * 128
                    inc(GP.indirect_dma_start(
                        out=bufd, out_offset=bass.IndirectOffsetOnAxis(ap=didx[:, pl0 + i:pl0 + i + 1], axis=0),
                        in_=xr[:, :], in_offset=None), "sg", 16)
                    inc(GP.indirect_dma_start(
                        out=bufd, out_offset=bass.IndirectOffsetOnAxis(ap=didx[:, pl1 + i:pl1 + i + 1], axis=0),
                        in_=xr[:, :], in_offset=None), "sg", 16)
                SG_D1 = cnt["sg"]

                _ckpt(40)
                ipz_bf = E2(nc.sbuf_tensor("ipz_bf", [128, 8 * DI], BF16))
                outw_bf = E2(nc.sbuf_tensor("outw_bf", [128, 16 * DM], BF16))
                wst2 = E2(nc.sbuf_tensor("wst2", [128, DI], F32))
                wt2 = E2(nc.sbuf_tensor("wt2", [128, NH * 128], BF16))
                xt2 = E2(nc.sbuf_tensor("xt2", [128, NH * 128], BF16))
                s_bf = E2(nc.sbuf_tensor("s_bf", [64, NH * HD], BF16))
                clam = E2(nc.sbuf_tensor("clam", [64, 128], BF16))
                clamg = E2(nc.sbuf_tensor("clamg", [64, 128], BF16))
                z_ch = E2(nc.sbuf_tensor("z_ch", [128, DI], BF16))
                y_ch = E2(nc.sbuf_tensor("y_ch", [128, DI], F32))
                yg = E2(nc.sbuf_tensor("yg", [128, DI], BF16))
                ygfm = E2(nc.sbuf_tensor("ygfm", [128, NH * 128], BF16))
                u_ch = E2(nc.sbuf_tensor("u_ch", [128, DM], F32))
                rstd2 = E2(nc.sbuf_tensor("rstd2", [128, 2], F32))

                for d in range(8):
                    w(SY, "sa")
                    inc(SY.dma_start(out=wst2[:, :],
                                     in_=ipw_t.rearrange("(a p) e -> p a e", p=128)[:, d, 0:DI]), "sd", 16)
                    w(SC, "sd")
                    inc(SC.activation(out=ipz_bf[:, d * DI:(d + 1) * DI], in_=wst2[:, :], func=AF.Copy), "sa")
                for ic in range(16):
                    w(SY, "sv")
                    w(SY, "sa")
                    inc(SY.dma_start(out=wst2[:, 0:DM],
                                     in_=outw_t.rearrange("(a p) e -> p a e", p=128)[:, ic, :]), "sd", 16)
                    w(VE, "sd")
                    inc(VE.tensor_scalar(outw_bf[:, ic * DM:(ic + 1) * DM], wst2[:, 0:DM],
                                         rmsw[:, ic:ic + 1], None, op0=OP.mult), "sv")

                for c in range(NTCH):
                    cslice = slice(c * 128, (c + 1) * 128)
                    w(SY, "sp")
                    inc(SY.dma_start(out=wt2[:, :], in_=wt_sp[c]), "sd", 16)
                    inc(SY.dma_start(out=xt2[:, :], in_=xtm_sp[c]), "sd", 16)
                    w(SY, "sv")
                    w(SY, "sp")
                    inc(SY.dma_start(out=s_bf[:, :], in_=s_sp[c]), "sd", 16)
                    inc(SY.dma_start(out=u_ch[:, :],
                                     in_=u_sl[HALO + c * 128: HALO + (c + 1) * 128, :]), "sd", 16)

                    for eb in range(4):
                        pp = p_b if eb % 2 == 0 else p_c
                        w(TE, "sv")
                        for d in range(8):
                            instr = TE.matmul(pp[:, :],
                                              xn_fm[:, d * TH + HALO + c * 128: d * TH + HALO + (c + 1) * 128],
                                              ipz_bf[:, d * DI + eb * 512: d * DI + (eb + 1) * 512],
                                              start=(d == 0), stop=(d == 7))
                        inc(instr, "sp")
                        w(VE, "sp")
                        inc(VE.tensor_copy(z_ch[:, eb * 512:(eb + 1) * 512], pp[:, :]), "sv", sw=False)

                    for h in range(NH):
                        hs = slice(h * 128, (h + 1) * 128)
                        w(TE, "sv")
                        instr = TE.matmul(p_s[0:64, 0:128], e64b[:, h * 64:(h + 1) * 64],
                                          lam_bf[:, cslice], start=True, stop=True)
                        inc(instr, "sp")
                        instr = TE.matmul(p_s[0:64, 128:256], e64b[:, h * 64:(h + 1) * 64],
                                          lamg_bf[:, cslice], start=True, stop=True)
                        inc(instr, "sp")
                        w(VE, "sp")
                        inc(VE.tensor_tensor(clam[:, :], cfm_all[:, cslice], p_s[0:64, 0:128],
                                             op=OP.mult), "sv", sw=False)
                        inc(VE.tensor_tensor(clamg[:, :], cfm_all[:, cslice], p_s[0:64, 128:256],
                                             op=OP.mult), "sv", sw=False)
                        w(TE, "sv")
                        grp = h % 4
                        TE.matmul(p_y[:, grp * 128:(grp + 1) * 128], wt2[:, hs], xt2[:, hs],
                                  start=True, stop=False)
                        TE.matmul(p_y[:, grp * 128:(grp + 1) * 128], clam[:, :], s_bf[:, hs],
                                  start=False, stop=False)
                        instr = TE.matmul(p_y[:, grp * 128:(grp + 1) * 128], clamg[:, :], sin_bf[:, hs],
                                          start=False, stop=True)
                        inc(instr, "sp")
                        if grp == 3:
                            g = h // 4
                            w(VE, "sp")
                            inc(VE.tensor_copy(y_ch[:, g * 512:(g + 1) * 512], p_y[:, :]), "sv", sw=False)
                            w(TE, "sv")

                    if False:
                        pass
                    w(SC, "sv")
                    inc(SC.activation(out=yg[:, :], in_=y_ch[:, :], func=AF.Square), "sa")
                    inc(SC.activation(out=z_ch[:, :], in_=z_ch[:, :], func=AF.Sigmoid), "sa")
                    w(VE, "sa")
                    inc(VE.reduce_sum(rstd2[:, 0:1], yg[:, :], axis=AXX), "sv")
                    w(SC, "sv")
                    inc(SC.activation(out=rstd2[:, 1:2], in_=rstd2[:, 0:1], func=AF.Sqrt,
                                      scale=1.0 / DI, bias=epsb[:, 0:1]), "sa")
                    w(VE, "sa")
                    inc(VE.reciprocal(rstd2[:, 0:1], rstd2[:, 1:2]), "sv")
                    w(SC, "sv")
                    inc(SC.activation(out=y_ch[:, :], in_=y_ch[:, :], func=AF.Copy,
                                      scale=rstd2[:, 0:1]), "sa")
                    w(VE, "sa")
                    inc(VE.tensor_tensor(yg[:, :], y_ch[:, :], z_ch[:, :], op=OP.mult), "sv")

                    for h in range(NH):
                        sl4 = (h % 4) * 128
                        if h % 4 == 0:
                            w(TE, "sv")
                        instr = TE.transpose(p_a[:, sl4:sl4 + 128], yg[:, h * 128:(h + 1) * 128],
                                             identb[:, :])
                        if h % 4 == 3:
                            inc(instr, "sp")
                            w(VE, "sp")
                            inc(VE.tensor_copy(ygfm[:, (h - 3) * 128:(h + 1) * 128], p_a[:, :]), "sv")

                    for db in range(2):
                        pp = p_b if db == 0 else p_c
                        w(TE, "sv")
                        for ic in range(16):
                            instr = TE.matmul(pp[:, :], ygfm[:, ic * 128:(ic + 1) * 128],
                                              outw_bf[:, ic * DM + db * 512: ic * DM + (db + 1) * 512],
                                              start=(ic == 0), stop=(ic == 15))
                        inc(instr, "sp")
                        w(VE, "sp")
                        inc(VE.tensor_tensor(y_ch[:, db * 512:(db + 1) * 512], pp[:, :],
                                             u_ch[:, db * 512:(db + 1) * 512], op=OP.add), "sv")
                    w(SY, "sv")
                    inc(SY.dma_start(out=uattn[c * 128:(c + 1) * 128, :], in_=y_ch[:, 0:DM]), "sd", 16)

                barrier(TE, VE, SC, GP, SY)
                _close(st2)
                _close(stAB)
                _ckpt(45)

                _ckpt(50)
                st3 = _mkstack()
                E3 = st3.enter_context
                w_in_bf = E3(nc.sbuf_tensor("w_in_bf", [128, 8 * 2 * FF], BF16))
                w_out_bf = E3(nc.sbuf_tensor("w_out_bf", [128, 16 * DM], BF16))
                wst3 = E3(nc.sbuf_tensor("wst3", [128, FF], F32))
                wst3b = E3(nc.sbuf_tensor("wst3b", [128, FF], F32))
                ibuf0 = E3(nc.sbuf_tensor("ibuf0", [128, DM], BF16))
                ibuf1 = E3(nc.sbuf_tensor("ibuf1", [128, DM], BF16))
                inp_fm = E3(nc.sbuf_tensor("inp_fm", [128, 8 * 128], BF16))
                hid2 = E3(nc.sbuf_tensor("hid2", [128, FF], BF16))
                h2fm = E3(nc.sbuf_tensor("h2fm", [128, 16 * 128], BF16))
                gelt = E3(nc.sbuf_tensor("gelt", [128, 512], F32))
                gelt2 = E3(nc.sbuf_tensor("gelt2", [128, 512], F32))
                outeb = E3(nc.sbuf_tensor("outeb", [128, DM], BF16))

                el_cs = {}
                for j in range(16):
                    d, hh = j // 2, j % 2
                    wb = wst3 if j % 2 == 0 else wst3b
                    if j >= 2:
                        w(SY, "sa", el_cs[j - 2])
                    else:
                        w(SY, "sa")
                    inc(SY.dma_start(out=wb[:, :],
                                     in_=w_in_d.rearrange("(a p) e -> p a e", p=128)[:, d, hh * FF:(hh + 1) * FF]), "sd", 16)
                    w(SC, "sd", cnt["sd"])
                    inc(SC.activation(out=w_in_bf[:, d * 2 * FF + hh * FF: d * 2 * FF + (hh + 1) * FF],
                                      in_=wb[:, :], func=AF.Copy), "sa")
                    el_cs[j] = cnt["sa"]
                fl_cs = {}
                for f in range(16):
                    wb = wst3 if f % 2 == 0 else wst3b
                    if f >= 2:
                        w(SY, "sa", fl_cs[f - 2])
                    else:
                        w(SY, "sa", el_cs[15])
                    inc(SY.dma_start(out=wb[:, 0:DM],
                                     in_=w_out_d.rearrange("(a p) e -> p a e", p=128)[:, f, :]), "sd", 16)
                    w(SC, "sd", cnt["sd"])
                    inc(SC.activation(out=w_out_bf[:, f * DM:(f + 1) * DM], in_=wb[:, 0:DM],
                                      func=AF.Copy), "sa")
                    fl_cs[f] = cnt["sa"]

                w(SY, "sg", SG_D1)
                if KDBG2:
                    inc(SY.dma_start(out=ibuf0[:, :], in_=bufd[0:128, :]), "sd", 16)
                    dcap(1536, ibuf0[:, 0:512], 512)
                    dcap(2048, ibuf0[:, 512:1024], 512)
                for ct in range(NCT):
                    ib = ibuf0 if ct % 2 == 0 else ibuf1
                    if ct >= 2:
                        w(SY, "sp")
                    inc(SY.dma_start(out=ib[:, :], in_=bufd[ct * 128:(ct + 1) * 128, :]), "sd", 16)
                    w(TE, "sd")
                    w(TE, "sv")
                    for d in range(8):
                        sl4 = (d % 4) * 128
                        instr = TE.transpose(p_a[:, sl4:sl4 + 128], ib[:, d * 128:(d + 1) * 128],
                                             identb[:, :])
                        if d % 4 == 3:
                            inc(instr, "sp")
                            w(VE, "sp")
                            inc(VE.tensor_copy(inp_fm[:, (d - 3) * 128:(d + 1) * 128], p_a[:, :]), "sv", sw=False)
                            w(TE, "sv")
                    sa_mark = {}
                    sv_mark = {}
                    for jf in range(4):
                        pa = p_b if jf % 2 == 0 else p_y
                        pg = p_c if jf % 2 == 0 else p_s
                        ge = gelt if jf % 2 == 0 else gelt2
                        if jf < 2:
                            w(TE, "sv")
                            w(TE, "sa")
                        else:
                            w(TE, "sa", sa_mark[jf - 2])
                            w(TE, "sv", sv_mark[jf - 2])
                        for d in range(8):
                            instr = TE.matmul(pa[:, 0:512], inp_fm[:, d * 128:(d + 1) * 128],
                                              w_in_bf[:, d * 2 * FF + jf * 512: d * 2 * FF + (jf + 1) * 512],
                                              start=(d == 0), stop=(d == 7))
                        inc(instr, "sp")
                        for d in range(8):
                            instr = TE.matmul(pg[:, 0:512], inp_fm[:, d * 128:(d + 1) * 128],
                                              w_in_bf[:, d * 2 * FF + FF + jf * 512: d * 2 * FF + FF + (jf + 1) * 512],
                                              start=(d == 0), stop=(d == 7))
                        inc(instr, "sp")
                        w(SC, "sp", cnt["sp"] - 1)
                        inc(SC.activation(out=ge[:, :], in_=pa[:, 0:512], func=AF.Gelu), "sa")
                        sa_mark[jf] = cnt["sa"]
                        w(VE, "sa", sa_mark[jf])
                        w(VE, "sp")
                        inc(VE.tensor_tensor(hid2[:, jf * 512:(jf + 1) * 512], ge[:, :],
                                             pg[:, 0:512], op=OP.mult), "sv", sw=False)
                        sv_mark[jf] = cnt["sv"]
                    for f in range(16):
                        sl4 = (f % 4) * 128
                        if f % 4 == 0:
                            w(TE, "sv")
                        instr = TE.transpose(p_a[:, sl4:sl4 + 128], hid2[:, f * 128:(f + 1) * 128],
                                             identb[:, :])
                        if f % 4 == 3:
                            inc(instr, "sp")
                            w(VE, "sp")
                            inc(VE.tensor_copy(h2fm[:, (f - 3) * 128:(f + 1) * 128], p_a[:, :]), "sv", sw=False)
                    for db in range(2):
                        pp = p_b if db == 0 else p_c
                        w(TE, "sv")
                        for f in range(16):
                            instr = TE.matmul(pp[:, :], h2fm[:, f * 128:(f + 1) * 128],
                                              w_out_bf[:, f * DM + db * 512: f * DM + (db + 1) * 512],
                                              start=(f == 0), stop=(f == 15))
                        inc(instr, "sp")
                        w(SC, "sp")
                        inc(SC.activation(out=outeb[:, db * 512:(db + 1) * 512], in_=pp[:, :],
                                          func=AF.Copy), "sa")
                    w(SY, "sa")
                    inc(SY.dma_start(out=pool_sl[ct * 128:(ct + 1) * 128, :], in_=outeb[:, :]), "sd", 16)

                _ckpt(60)
                w(GP, "sd")
                inc(GP.collective_compute("AllGather", OP.bypass,
                                          replica_groups=[list(range(NCORES))],
                                          ins=[pool_sl.opt()], outs=[pool_all.opt()]), "sc")
                AG4_C = cnt["sc"]

                r0 = E3(nc.sbuf_tensor("r0", [128, DM], BF16))
                r1 = E3(nc.sbuf_tensor("r1", [128, DM], BF16))
                ffn = E3(nc.sbuf_tensor("ffn", [128, DM], F32))
                ua2 = E3(nc.sbuf_tensor("ua2", [128, DM], F32))
                if KDBG2:
                    w(SY, "sc")
                    inc(SY.dma_start(out=ibuf0[:, :], in_=pool_sl[0:128, :]), "sd", 16)
                    dcap(3712, ibuf0[:, 0:192], 192)
                    inc(SY.dma_start(out=ibuf0[:, :], in_=pool_all[0:128, :]), "sd", 16)
                    dcap(3904, ibuf0[:, 0:192], 192)
                inc(GP.memset(r0[:, :], 0.0), "sq")
                inc(GP.memset(r1[:, :], 0.0), "sq")
                w(GP, "sc", AG4_C)
                for k in range(NTCH):
                    if k > 0:
                        w(GP, "sv")
                    inc(GP.indirect_dma_start(out=r0[:, :], out_offset=None, in_=pool_all,
                                              in_offset=bass.IndirectOffsetOnAxis(ap=cgi[:, k:k + 1], axis=0)), "sg", 16)
                    w(GP, "sg")
                    inc(GP.indirect_dma_start(out=r1[:, :], out_offset=None, in_=pool_all,
                                              in_offset=bass.IndirectOffsetOnAxis(ap=cgi[:, NTCH + k:NTCH + k + 1], axis=0)), "sg", 16)
                    w(GP, "sg")
                    w(SY, "sv")
                    inc(SY.dma_start(out=ua2[:, :], in_=uattn[k * 128:(k + 1) * 128, :]), "sd", 16)
                    w(VE, "sg")
                    w(VE, "sd")
                    w(VE, "sq")
                    inc(VE.tensor_scalar(ffn[:, :], r0[:, :], ga[:, k * 2:k * 2 + 1], None,
                                         op0=OP.mult), "sv")
                    inc(VE.scalar_tensor_tensor(ffn[:, :], r1[:, :], ga[:, k * 2 + 1:k * 2 + 2],
                                                ffn[:, :], op0=OP.mult, op1=OP.add), "sv")
                    inc(VE.tensor_tensor(ffn[:, :], ffn[:, :], ua2[:, :], op=OP.add), "sv")
                    if k == 0:
                        dcap(2688, r0[:, 0:512], 512)
                        dcap(3200, ffn[:, 0:256], 256)
                    w(SY, "sv")
                    inc(SY.dma_start(out=yout[k * 128:(k + 1) * 128, :], in_=ffn[:, :]), "sd", 16)

                w(SY, "sd")
                w(SY, "sg")
                _close(st3)
            except _StopEmit:
                if _os.environ.get("KDBG") == "uattn" and KPH >= 41:
                    barrier(TE, VE, SC, GP, SY)
                    for k in range(NTCH):
                        inc(SY.dma_start(out=yout[k * 128:(k + 1) * 128, :],
                                         in_=uattn[k * 128:(k + 1) * 128, :]), "sd", 16)
                barrier(TE, VE, SC, GP, SY)
                w(SY, "sc")
                for _s in reversed(_open):
                    _s.close()
                _open.clear()
    return nc



_NC_CACHE = {}


def _get_nc():
    if "nc" not in _NC_CACHE:
        _NC_CACHE["nc"] = build_nc()
    return _NC_CACHE["nc"]


def _stage(inputs):
    bf = ml_dtypes.bfloat16
    f32 = np.float32
    u = np.ascontiguousarray(inputs["L0_attn_u"], dtype=f32).reshape(NCORES * TOK, DM)
    ipw = np.ascontiguousarray(np.asarray(inputs["L0_attn_in_proj_w"], dtype=f32).T)
    outw = np.ascontiguousarray(np.asarray(inputs["L0_attn_out_w"], dtype=f32).T)
    cw = np.asarray(inputs["L0_attn_conv_w"], dtype=f32)
    cb = np.asarray(inputs["L0_attn_conv_b"], dtype=f32)
    convw = np.ascontiguousarray(cw.reshape(NEC, 128, 4).transpose(1, 0, 2).reshape(128, NEC * 4))
    convb = np.ascontiguousarray(cb.reshape(NEC, 128).T)
    dtb = np.asarray(inputs["L0_attn_dt_bias"], dtype=f32).reshape(NH, 1)
    alog = np.asarray(inputs["L0_attn_A_log"], dtype=f32).reshape(NH, 1)
    dvec = np.asarray(inputs["L0_attn_D"], dtype=f32).reshape(1, NH)
    rmsw = np.ascontiguousarray(np.asarray(inputs["L0_attn_rms_w"], dtype=f32).reshape(NH, 128).T)
    wg = np.ascontiguousarray(np.asarray(inputs["L0_ffn_Wg"], dtype=f32))
    w_in = np.asarray(inputs["L0_ffn_W_in"], dtype=f32)
    w_out = np.asarray(inputs["L0_ffn_W_out"], dtype=f32)

    idx = np.arange(128)
    common = dict(
        ltmask=np.where(idx[:, None] <= idx[None, :], 0.0, -3.0e38).astype(f32),
        ltri=(idx[:, None] <= idx[None, :]).astype(f32),
        identb=np.eye(128).astype(bf),
        identf=np.eye(128).astype(f32),
        i16=np.eye(16, dtype=f32),
        ones1f=np.ones((1, 128), f32),
        onescol=np.ones((128, 1), f32),
        ones16_64=np.ones((16, 64), f32),
        iota8=np.broadcast_to(np.arange(8, dtype=f32), (128, 8)).copy(),
        iota8b=(np.broadcast_to(np.arange(8, dtype=f32), (128, 8)) + BIGI).astype(f32),
        convw=convw, convb=convb, dtb_col=dtb, alog_col=alog, dvec=dvec,
        rmsw_col=rmsw, wg=wg, ipw_t=ipw, outw_t=outw,
    )
    e16f = np.zeros((16, 16 * 128), f32)
    e16n = np.zeros((16, 16 * 128), f32)
    e64 = np.zeros((16, 16 * 64), f32)
    for h in range(16):
        e16f[h, h * 128:(h + 1) * 128] = 1.0
        e16n[h, h * 128:(h + 1) * 128] = -1.0
        e64[h, h * 64:(h + 1) * 64] = 1.0
    common["e16f"] = e16f
    common["e16n"] = e16n
    common["e64b"] = e64.astype(bf)

    in_maps = []
    for r in range(NCORES):
        m = dict(common)
        start = r * TOK
        if r % 2 == 0:
            usl = np.concatenate([np.zeros((HALO, DM), f32), u[start:start + TOK]], axis=0)
        else:
            usl = u[start - HALO:start + TOK]
        m["u_sl"] = np.ascontiguousarray(usl)
        m["w_in_e"] = np.ascontiguousarray(w_in[r])
        m["w_out_e"] = np.ascontiguousarray(w_out[r])
        pm = np.zeros((8, 2), f32)
        pm[:r, 0] = 1.0
        pm[:, 1] = 1.0
        m["prevmask"] = pm
        m["sgidx"] = (np.arange(64, dtype=np.uint32).reshape(64, 1)
                      + np.uint32(((r - 1) % NCORES) * 64))
        m["szero"] = np.full((64, 1), 1.0 if r % 2 == 1 else 0.0, f32)
        m["ecvec"] = np.full((128, 1), float(r * CAP), f32)
        for j in range(4):
            m[f"qtr{j}"] = (np.arange(128, dtype=f32) + CAP + j * 128).reshape(128, 1)
        m["qpool"] = (np.arange(128, dtype=f32) + (NCORES * CAP - 128)).reshape(128, 1)
        in_maps.append(m)
    return in_maps


def kernel(**inputs):
    nc = _get_nc()
    in_maps = _stage(inputs)
    res = run_bass_kernel_spmd(nc, in_maps, core_ids=list(range(NCORES)))
    globals()["_LAST_RES"] = res
    out = np.concatenate([res.results[r]["yout"] for r in range(NCORES)], axis=0)
    return out.reshape(4, 4096, DM)
